# version 07
# baseline (speedup 1.0000x reference)
import math
import numpy as np
import ml_dtypes
from contextlib import ExitStack

import concourse.bass as bass
import concourse.mybir as mybir
from concourse.bass import broadcast_tensor_aps
from concourse.bass_utils import run_bass_kernel_spmd

F32 = mybir.dt.float32
BF16 = mybir.dt.bfloat16
AF = mybir.ActivationFunctionType
AX = mybir.AxisListType.X

D = 4096
S = 4096
B = 4
HD = 128
NKV = 8
NOWN = 2048
NCH = 16
DT = 32
JW = 48
ISQ = 1.0 / math.sqrt(HD)
MBIG = -30000.0
EPS = 1e-5


def _build(dbg=frozenset()):
    nc = bass.Bass("TRN2", target_bir_lowering=False)

    def dram(name, shape, dt, kind):
        if kind == "Internal" and name in dbg:
            kind = "ExternalOutput"
        return nc.dram_tensor(name, shape, dt, kind=kind).ap()

    xT = dram("xT", [D, S], F32, "ExternalInput")
    wqT = dram("wqT", [D, 6144], F32, "ExternalInput")
    woT = dram("woT", [D, D], F32, "ExternalInput")
    wfT = dram("wfT", [D, D], F32, "ExternalInput")
    gwT = dram("gwT", [D, 8], F32, "ExternalInput")
    ics8 = dram("ics8", [8, D], F32, "ExternalInput")
    ocs8 = dram("ocs8", [8, D], F32, "ExternalInput")
    bqkv = dram("bqkv", [128, JW], F32, "ExternalInput")
    bo = dram("bo", [128, DT], F32, "ExternalInput")
    bff = dram("bff", [128, DT], F32, "ExternalInput")
    cost = dram("cost", [128, S], F32, "ExternalInput")
    sint = dram("sint", [128, S], F32, "ExternalInput")
    mtri = dram("mtri", [128, 512], BF16, "ExternalInput")
    moth = dram("moth", [128, 512], BF16, "ExternalInput")
    perm = dram("perm", [128, 128], BF16, "ExternalInput")
    iden = dram("iden", [128, 128], BF16, "ExternalInput")
    onec = dram("onec", [128, 1], BF16, "ExternalInput")
    onerow = dram("onerow", [1, 128], F32, "ExternalInput")
    onerow_bf = dram("onerow_bf", [1, 128], BF16, "ExternalInput")
    yT = dram("yT", [D, NOWN], F32, "ExternalOutput")

    xnT_scr = dram("xnT_scr", [D, S], BF16, "Internal")
    q_scr = dram("q_scr", [D, NOWN], BF16, "Internal")
    k_scr = dram("k_scr", [NKV * HD, S], BF16, "Internal")
    v_scr = dram("v_scr", [NKV * HD, S], BF16, "Internal")
    outT_scr = dram("outT_scr", [D, NOWN], BF16, "Internal")
    projT_scr = dram("projT_scr", [D, NOWN], F32, "Internal")
    xsT_scr = dram("xsT_scr", [D, NOWN], BF16, "Internal")

    xnT_r = xnT_scr.rearrange("(dt part) s -> part dt s", part=128)
    xT_r = xT.rearrange("(dt part) s -> part dt s", part=128)
    wqT_r = wqT.rearrange("(dt part) j -> part dt j", part=128)
    woT_r = woT.rearrange("(dt part) j -> part dt j", part=128)
    wfT_r = wfT.rearrange("(dt part) j -> part dt j", part=128)
    gwT_r = gwT.rearrange("(dt part) j -> part dt j", part=128)
    q_scr_r = q_scr.rearrange("(h part) s -> part h s", part=128)
    outT_r = outT_scr.rearrange("(dt part) s -> part dt s", part=128)
    xsT_r = xsT_scr.rearrange("(dt part) s -> part dt s", part=128)

    with ExitStack() as top:
        E = top.enter_context
        bqkv_sb = E(nc.sbuf_tensor("bqkv_sb", [128, JW], F32))
        bo_sb = E(nc.sbuf_tensor("bo_sb", [128, DT], F32))
        bff_sb = E(nc.sbuf_tensor("bff_sb", [128, DT], F32))
        bsum_sb = E(nc.sbuf_tensor("bsum_sb", [128, DT], F32))
        iden_sb = E(nc.sbuf_tensor("iden_sb", [128, 128], BF16))
        onec_sb = E(nc.sbuf_tensor("onec_sb", [128, 1], BF16))
        onerow_sb = E(nc.sbuf_tensor("onerow_sb", [1, 128], F32))
        onerow_bf_sb = E(nc.sbuf_tensor("onerow_bf_sb", [1, 128], BF16))
        ocs_sb = E(nc.sbuf_tensor("ocs_sb", [8, D], BF16))
        routT_sb = E(nc.sbuf_tensor("routT_sb", [8, NOWN], BF16))
        negB_sb = E(nc.sbuf_tensor("negB_sb", [128, 1], F32))
        cst_sb = E(nc.sbuf_tensor("cst_sb", [1, 8], F32))

        sB = E(nc.semaphore("sB"))
        vSelf = E(nc.semaphore("vSelf"))
        sSelf = E(nc.semaphore("sSelf"))
        SELF = {"v": 0, "s": 0}

        def vedge(instr):
            SELF["v"] += 1
            instr.then_inc(vSelf, 1)
            return SELF["v"]

        def sedge(instr):
            SELF["s"] += 1
            instr.then_inc(sSelf, 1)
            return SELF["s"]

        PH = {"k": 0}

        def phase_wait(e):
            e.wait_ge(sB, 5 * PH["k"])

        def phase_inc(e):
            e.sem_inc(sB, 1)

        s_g = E(nc.semaphore("s_g"))
        s_g2 = E(nc.semaphore("s_g2"))
        with nc.Block() as blk:
            @blk.sync
            def _(s):
                for dst, src in [
                    (bqkv_sb, bqkv), (bo_sb, bo), (bff_sb, bff),
                    (iden_sb, iden), (onec_sb, onec), (onerow_sb, onerow),
                    (onerow_bf_sb, onerow_bf),
                ]:
                    s.dma_start(dst[:], src).then_inc(s_g, 16)
                phase_inc(s)

            @blk.gpsimd
            def _(g):
                g.dma_start(ocs_sb[:], ocs8).then_inc(s_g2, 16)
                phase_inc(g)

            @blk.vector
            def _(v):
                v.wait_ge(s_g, 112)
                v.wait_ge(s_g2, 16)
                v.tensor_add(bsum_sb[:], bo_sb[:], bff_sb[:])
                v.memset(cst_sb[0:1, 0:1], EPS)
                v.memset(cst_sb[0:1, 1:2], math.log(ISQ))
                v.memset(cst_sb[0:1, 2:3], 64.0 * math.log(2.0))
                phase_inc(v)

            @blk.tensor
            def _(t):
                phase_inc(t)

            @blk.scalar
            def _(sc):
                phase_inc(sc)
        PH["k"] += 1

        NG = 16
        GSZ = 256
        s_x = [E(nc.semaphore(f"s_x{i}")) for i in range(2)]
        s_v1 = E(nc.semaphore("s_v1"))
        s_sq1 = E(nc.semaphore("s_sq1"))
        s_t1 = E(nc.semaphore("s_t1"))
        s_v2 = E(nc.semaphore("s_v2"))
        s_v3 = E(nc.semaphore("s_v3"))
        s_sc0 = E(nc.semaphore("s_sc0"))
        s_v4 = E(nc.semaphore("s_v4"))
        s_t2 = E(nc.semaphore("s_t2"))
        s_sc1 = E(nc.semaphore("s_sc1"))
        s_v5 = E(nc.semaphore("s_v5"))
        s_sp = [E(nc.semaphore(f"s_sp{i}")) for i in range(2)]
        with ExitStack() as ph:
            PE = ph.enter_context
            xg = [PE(nc.sbuf_tensor(f"xg{i}", [128, DT, GSZ], F32)) for i in range(2)]
            xb = [PE(nc.sbuf_tensor(f"xb{i}", [128, DT, GSZ], BF16)) for i in range(2)]
            sq = [PE(nc.sbuf_tensor(f"sq{i}", [128, DT, GSZ], BF16)) for i in range(2)]
            xn = [PE(nc.sbuf_tensor(f"xn{i}", [128, DT, GSZ], BF16)) for i in range(2)]
            sb_s = [PE(nc.sbuf_tensor(f"sb_s{i}", [128, 1, GSZ], BF16)) for i in range(2)]
            sb_ms = [PE(nc.sbuf_tensor(f"sb_ms{i}", [128, 1, GSZ], BF16)) for i in range(2)]
            m_row = PE(nc.sbuf_tensor("m_row", [1, GSZ], F32))
            ex2_row = PE(nc.sbuf_tensor("ex2_row", [1, GSZ], F32))
            m2_row = PE(nc.sbuf_tensor("m2_row", [1, GSZ], F32))
            var_row = PE(nc.sbuf_tensor("var_row", [1, GSZ], F32))
            ln_row = PE(nc.sbuf_tensor("ln_row", [1, GSZ], F32))
            rstd_row = PE(nc.sbuf_tensor("rstd_row", [1, GSZ], F32))
            ms_row = PE(nc.sbuf_tensor("ms_row", [1, GSZ], F32))
            ps_sx = PE(nc.psum_tensor("ps_sx", [1, GSZ], F32))
            ps_sx2 = PE(nc.psum_tensor("ps_sx2", [1, GSZ], F32))
            ps_b1 = PE(nc.psum_tensor("ps_b1", [128, GSZ], F32))
            ps_b2 = PE(nc.psum_tensor("ps_b2", [128, GSZ], F32))
            with nc.Block() as blk:
                @blk.sync
                def _(s):
                    phase_wait(s)
                    for g in range(NG):
                        if g >= 2:
                            s.wait_ge(s_v1, g - 1)
                            s.wait_ge(s_sq1, g - 1)
                            s.wait_ge(s_x[g % 2], 16 * (g // 2))
                        s.dma_start(
                            xg[g % 2][:], xT_r[:, :, g * GSZ:(g + 1) * GSZ]
                        ).then_inc(s_x[g % 2], 16)
                    phase_inc(s)

                @blk.vector
                def _(v):
                    phase_wait(v)
                    for g in range(NG + 1):
                        if g < NG:
                            v.wait_ge(s_x[g % 2], 16 * (g // 2 + 1))
                            if g >= 2:
                                v.wait_ge(s_t1, g - 1)
                            v.tensor_copy(xb[g % 2][:], xg[g % 2][:]).then_inc(s_v1, 1)
                        if g >= 1:
                            p = g - 1
                            v.wait_ge(s_t1, p + 1)
                            na = vedge(v.tensor_scalar_mul(m_row[:], ps_sx[:], 1.0 / D))
                            v.tensor_scalar_mul(ex2_row[:], ps_sx2[:],
                                                1.0 / D).then_inc(s_v2, 1)
                            v.wait_ge(vSelf, na)
                            n1 = vedge(v.tensor_mul(m2_row[:], m_row[:], m_row[:]))
                            v.wait_ge(s_v2, p + 1)
                            v.wait_ge(vSelf, n1)
                            v.tensor_sub(var_row[:], ex2_row[:], m2_row[:]).then_inc(s_v3, 1)
                            v.wait_ge(s_sc0, p + 1)
                            v.tensor_mul(ms_row[:], m_row[:], rstd_row[:]).then_inc(s_v4, 1)
                            v.wait_ge(s_sc1, p + 1)
                            if p >= 2:
                                v.wait_ge(s_sp[p % 2], 16 * (p // 2))
                            a1, b1 = broadcast_tensor_aps(xb[p % 2][:], sb_s[p % 2][:])
                            n2 = vedge(v.tensor_mul(xn[p % 2][:], a1, b1))
                            v.wait_ge(vSelf, n2)
                            a2, b2 = broadcast_tensor_aps(xn[p % 2][:], sb_ms[p % 2][:])
                            v.tensor_sub(xn[p % 2][:], a2, b2).then_inc(s_v5, 1)
                    phase_inc(v)

                @blk.scalar
                def _(sc):
                    phase_wait(sc)
                    for g in range(NG + 1):
                        if g < NG:
                            sc.wait_ge(s_x[g % 2], 16 * (g // 2 + 1))
                            if g >= 2:
                                sc.wait_ge(s_t1, g - 1)
                            sc.activation(sq[g % 2][:], xg[g % 2][:],
                                          AF.Square).then_inc(s_sq1, 1)
                        if g >= 1:
                            p = g - 1
                            sc.wait_ge(s_v3, p + 1)
                            if p >= 1:
                                sc.wait_ge(s_t2, p)
                            m1 = sedge(sc.activation(ln_row[:], var_row[:], AF.Ln,
                                                     bias=cst_sb[0:1, 0:1]))
                            sc.wait_ge(sSelf, m1)
                            sc.activation(rstd_row[:], ln_row[:], AF.Exp,
                                          scale=-0.5).then_inc(s_sc0, 1)
                            sc.wait_ge(s_t2, p + 1)
                            sc.copy(sb_s[p % 2][:, 0, :], ps_b1[:])
                            sc.copy(sb_ms[p % 2][:, 0, :], ps_b2[:]).then_inc(s_sc1, 1)
                    phase_inc(sc)

                @blk.tensor
                def _(t):
                    phase_wait(t)
                    for g in range(NG + 1):
                        if g < NG:
                            t.wait_ge(s_v1, g + 1)
                            t.wait_ge(s_sq1, g + 1)
                            if g >= 1:
                                t.wait_ge(s_v2, g)
                            for dt in range(DT):
                                t.matmul(ps_sx[:], onec_sb[:], xb[g % 2][:, dt, :],
                                         start=(dt == 0), stop=(dt == DT - 1))
                            last = None
                            for dt in range(DT):
                                last = t.matmul(ps_sx2[:], onec_sb[:], sq[g % 2][:, dt, :],
                                                start=(dt == 0), stop=(dt == DT - 1))
                            last.then_inc(s_t1, 1)
                        if g >= 1:
                            p = g - 1
                            t.wait_ge(s_v4, p + 1)
                            if p >= 1:
                                t.wait_ge(s_sc1, p)
                            t.matmul(ps_b1[:], onerow_sb[:], rstd_row[:], start=True, stop=True)
                            t.matmul(ps_b2[:], onerow_sb[:], ms_row[:],
                                     start=True, stop=True).then_inc(s_t2, 1)
                    phase_inc(t)

                @blk.gpsimd
                def _(g_):
                    phase_wait(g_)
                    for g in range(NG):
                        g_.wait_ge(s_v5, g + 1)
                        if g >= 2:
                            g_.wait_ge(s_sp[g % 2], 16 * (g // 2))
                        g_.dma_start(
                            xnT_r[:, :, g * GSZ:(g + 1) * GSZ], xn[g % 2][:]
                        ).then_inc(s_sp[g % 2], 16)
                    g_.wait_ge(s_sp[0], 16 * (NG // 2))
                    g_.wait_ge(s_sp[1], 16 * (NG // 2))
                    phase_inc(g_)
        PH["k"] += 1

        jobs = [(w, 0) for w in range(JW)] + [(w, 1) for w in range(32, JW)]
        bd_jobs = [i for i, (w, h) in enumerate(jobs) if w < 40]
        bd_rank = {j: r for r, j in enumerate(bd_jobs)}
        first_k_rank = min(r for r, j in enumerate(bd_jobs) if jobs[j][0] >= 32)
        s_xh = E(nc.semaphore("s_xh"))
        s_xh2 = E(nc.semaphore("s_xh2"))
        s_w2 = [E(nc.semaphore(f"s_w2{i}")) for i in range(3)]
        s_jt2 = E(nc.semaphore("s_jt2"))
        s_ev2 = E(nc.semaphore("s_ev2"))
        s_st2 = [E(nc.semaphore(f"s_st2{i}")) for i in range(2)]
        s_sqst = E(nc.semaphore("s_sqst"))
        s_bnd4 = E(nc.semaphore("s_bnd4"))
        s_bdr = E(nc.semaphore("s_bdr"))
        s_nbB = E(nc.semaphore("s_nbB"))
        with ExitStack() as ph:
            PE = ph.enter_context
            xnh = PE(nc.sbuf_tensor("xnh", [128, DT, NOWN], BF16))
            wcol = [PE(nc.sbuf_tensor(f"wcol{i}", [128, DT, 128], BF16)) for i in range(3)]
            stage = [PE(nc.sbuf_tensor(f"stage{i}", [128, NOWN], BF16)) for i in range(2)]
            sqst = PE(nc.sbuf_tensor("sqst", [128, NOWN], BF16))
            qmax_r = PE(nc.sbuf_tensor("qmax_r", [1, 1], F32))
            kmax_r = PE(nc.sbuf_tensor("kmax_r", [1, 1], F32))
            blkr = PE(nc.sbuf_tensor("blkr", [1, 1], F32))
            bbr = PE(nc.sbuf_tensor("bbr", [1, 1], F32))
            ps_q = [PE(nc.psum_tensor(f"ps_q{i}", [128, 2048], F32)) for i in range(2)]

            def p2_dest(w, half):
                cols = slice(half * 2048, (half + 1) * 2048)
                if w < 32:
                    return q_scr[w * 128:(w + 1) * 128, :]
                if w < 40:
                    return k_scr[(w - 32) * 128:(w - 31) * 128, cols]
                return v_scr[(w - 40) * 128:(w - 39) * 128, cols]

            with nc.Block() as blk:
                @blk.sync
                def _(s):
                    phase_wait(s)
                    for q4 in range(4):
                        s.dma_start(
                            xnh[:, :, q4 * 512:(q4 + 1) * 512],
                            xnT_r[:, :, q4 * 512:(q4 + 1) * 512],
                        ).then_inc(s_xh, 16)
                    s.wait_ge(s_jt2, JW)
                    s.wait_ge(s_xh, 64)
                    for q4 in range(4):
                        s.dma_start(
                            xnh[:, :, q4 * 512:(q4 + 1) * 512],
                            xnT_r[:, :, 2048 + q4 * 512:2048 + (q4 + 1) * 512],
                        ).then_inc(s_xh2, 16)
                    phase_inc(s)

                @blk.gpsimd
                def _(g):
                    phase_wait(g)
                    for idx, (w, half) in enumerate(jobs):
                        if idx >= 3:
                            g.wait_ge(s_jt2, idx - 2)
                            g.wait_ge(s_w2[idx % 3], 16 * (idx // 3))
                        g.dma_start(
                            wcol[idx % 3][:], wqT_r[:, :, w * 128:(w + 1) * 128]
                        ).then_inc(s_w2[idx % 3], 16)
                    phase_inc(g)

                @blk.tensor
                def _(t):
                    phase_wait(t)
                    for idx, (w, half) in enumerate(jobs):
                        t.wait_ge(s_w2[idx % 3], 16 * (idx // 3 + 1))
                        if idx < JW:
                            t.wait_ge(s_xh, 64)
                        else:
                            t.wait_ge(s_xh2, 64)
                        if idx >= 2:
                            t.wait_ge(s_ev2, idx - 1)
                            nred = sum(1 for j in bd_jobs if j <= idx - 2)
                            if nred:
                                t.wait_ge(s_bdr, nred)
                        last = None
                        for dt in range(DT):
                            for nb in range(4):
                                last = t.matmul(
                                    ps_q[idx % 2][:, nb * 512:(nb + 1) * 512],
                                    wcol[idx % 3][:, dt, :],
                                    xnh[:, dt, nb * 512:(nb + 1) * 512],
                                    start=(dt == 0), stop=(dt == DT - 1),
                                )
                        last.then_inc(s_jt2, 1)
                        pj = idx - 1
                        if pj in bd_rank:
                            r = bd_rank[pj]
                            t.wait_ge(s_ev2, pj + 1)
                            t.wait_ge(s_sqst, r + 1)
                            for nb in range(4):
                                t.matmul(ps_q[pj % 2][0:1, 0:512], onec_sb[:],
                                         sqst[:, nb * 512:(nb + 1) * 512],
                                         start=(nb == 0),
                                         stop=(nb == 3)).then_inc(s_bnd4, 1)
                    pj = len(jobs) - 1
                    if pj in bd_rank:
                        r = bd_rank[pj]
                        t.wait_ge(s_ev2, pj + 1)
                        t.wait_ge(s_sqst, r + 1)
                        for nb in range(4):
                            t.matmul(ps_q[pj % 2][0:1, 0:512], onec_sb[:],
                                     sqst[:, nb * 512:(nb + 1) * 512],
                                     start=(nb == 0), stop=(nb == 3)).then_inc(s_bnd4, 1)
                    t.wait_ge(s_ev2, len(jobs))
                    t.wait_ge(s_bdr, len(bd_jobs))
                    t.wait_ge(s_nbB, 3)
                    t.matmul(ps_q[0][:, 0:1], onerow_sb[:], bbr[:],
                             start=True, stop=True).then_inc(s_nbB, 1)
                    phase_inc(t)

                @blk.scalar
                def _(sc):
                    phase_wait(sc)
                    for idx, (w, half) in enumerate(jobs):
                        sc.wait_ge(s_jt2, idx + 1)
                        if idx >= 2:
                            sc.wait_ge(s_st2[idx % 2], 16 * (idx // 2))
                            nsq = sum(1 for j in bd_jobs if j <= idx - 2)
                            if nsq:
                                sc.wait_ge(s_sqst, nsq)
                        last = None
                        for nb in range(4):
                            last = sc.activation(
                                stage[idx % 2][:, nb * 512:(nb + 1) * 512],
                                ps_q[idx % 2][:, nb * 512:(nb + 1) * 512],
                                AF.Identity, bias=bqkv_sb[:, w:w + 1], scale=1.0,
                            )
                        last.then_inc(s_ev2, 1)
                        sc.wait_ge(s_ev2, idx + 1)
                        sc.dma_start(p2_dest(w, half),
                                     stage[idx % 2][:]).then_inc(s_st2[idx % 2], 16)
                    sc.wait_ge(s_st2[0], 16 * ((len(jobs) + 1) // 2))
                    sc.wait_ge(s_st2[1], 16 * (len(jobs) // 2))
                    sc.wait_ge(s_nbB, 1)
                    m1 = sedge(sc.activation(bbr[:], bbr[:], AF.Ln))
                    sc.wait_ge(sSelf, m1)
                    sc.activation(bbr[:], bbr[:], AF.Exp, scale=0.5,
                                  bias=cst_sb[0:1, 1:2]).then_inc(s_nbB, 1)
                    sc.wait_ge(s_nbB, 4)
                    sc.copy(negB_sb[:], ps_q[0][:, 0:1])
                    phase_inc(sc)

                @blk.vector
                def _(v):
                    phase_wait(v)
                    for r, j in enumerate(bd_jobs):
                        w, half = jobs[j]
                        v.wait_ge(s_ev2, j + 1)
                        if r >= 1:
                            v.wait_ge(s_bnd4, 4 * r)
                        v.tensor_mul(sqst[:], stage[j % 2][:],
                                     stage[j % 2][:]).then_inc(s_sqst, 1)
                        v.wait_ge(s_bnd4, 4 * (r + 1))
                        n1 = vedge(v.reduce_max(blkr[:], ps_q[j % 2][0:1, 0:512], axis=AX))
                        v.wait_ge(vSelf, n1)
                        if r == 0:
                            v.tensor_copy(qmax_r[:], blkr[:]).then_inc(s_bdr, 1)
                        elif r == first_k_rank:
                            v.tensor_copy(kmax_r[:], blkr[:]).then_inc(s_bdr, 1)
                        elif w < 32:
                            v.tensor_max(qmax_r[:], qmax_r[:], blkr[:]).then_inc(s_bdr, 1)
                        else:
                            v.tensor_max(kmax_r[:], kmax_r[:], blkr[:]).then_inc(s_bdr, 1)
                    v.wait_ge(s_bdr, len(bd_jobs))
                    v.tensor_mul(bbr[:], qmax_r[:], kmax_r[:]).then_inc(s_nbB, 1)
                    v.wait_ge(s_nbB, 2)
                    v.tensor_scalar_mul(bbr[:], bbr[:], -1.0).then_inc(s_nbB, 1)
                    phase_inc(v)
        PH["k"] += 1

        s_c3 = E(nc.semaphore("s_c3"))
        s_graw = E(nc.semaphore("s_graw"))
        s_qkraw = E(nc.semaphore("s_qkraw"))
        s_vtk = [E(nc.semaphore(f"s_vtk{i}")) for i in range(2)]
        s_S = E(nc.semaphore("s_S"))
        s_P = E(nc.semaphore("s_P"))
        s_qbD = E(nc.semaphore("s_qbD"))
        s_avc = E(nc.semaphore("s_avc"))
        s_r3 = E(nc.semaphore("s_r3"))
        s_rb = E(nc.semaphore("s_rb"))
        s_N = E(nc.semaphore("s_N"))
        s_og = E(nc.semaphore("s_og"))
        TRI_G = sum(2 * (i + 1) for i in range(NCH))
        with ExitStack() as ph:
            PE = ph.enter_context
            cost_sb = PE(nc.sbuf_tensor("cost_sb", [128, S], F32))
            sint_sb = PE(nc.sbuf_tensor("sint_sb", [128, S], F32))
            mtri_sb = PE(nc.sbuf_tensor("mtri_sb", [128, 512], BF16))
            moth_sb = PE(nc.sbuf_tensor("moth_sb", [128, 512], BF16))
            kraw = PE(nc.sbuf_tensor("kraw", [128, S], BF16))
            kswap = PE(nc.sbuf_tensor("kswap", [128, S], BF16))
            qraw = PE(nc.sbuf_tensor("qraw", [128, 4, NOWN], BF16))
            qswap = PE(nc.sbuf_tensor("qswap", [128, 4, NOWN], BF16))
            krope = [PE(nc.sbuf_tensor(f"krope{i}", [128, S], BF16)) for i in range(2)]
            qrope = [PE(nc.sbuf_tensor(f"qrope{i}", [128, NCH, 4, 128], BF16)) for i in range(2)]
            vtok = [PE(nc.sbuf_tensor(f"vtok{i}", [128, DT, 128], BF16)) for i in range(2)]
            outg = [PE(nc.sbuf_tensor(f"outg{i}", [128, NCH, 4, 128], BF16)) for i in range(2)]
            pt = PE(nc.sbuf_tensor("pt", [128, 6, 512], BF16))
            t1b = [PE(nc.sbuf_tensor(f"t1b{i}", [128, 512], BF16)) for i in range(2)]
            av_sb = [PE(nc.sbuf_tensor(f"av_sb{i}", [128, 512], F32)) for i in range(2)]
            dr_bf = [PE(nc.sbuf_tensor(f"dr_bf{i}", [1, 512], BF16)) for i in range(2)]
            dr_ln = PE(nc.sbuf_tensor("dr_ln", [1, 512], F32))
            ps_s = PE(nc.psum_tensor("ps_s", [128, 2048], F32))
            ps_av = [PE(nc.psum_tensor(f"ps_av{i}", [128, 512], F32)) for i in range(2)]
            ps_d = [PE(nc.psum_tensor(f"ps_d{i}", [1, 512], F32)) for i in range(2)]

            def kt_list(i):
                return [("own", j) for j in range(i + 1)] + [("oth", j) for j in range(i + 1)]

            def rope_srcs(k):
                if k < 8:
                    cols = slice(k * 512, (k + 1) * 512)
                    return kraw[:, cols], kswap[:, cols], cols
                h, bq = divmod(k - 8, 4)
                cols = slice(bq * 512, (bq + 1) * 512)
                return qraw[:, h, cols], qswap[:, h, cols], cols

            def rope_dst(g_, k):
                if k < 8:
                    return krope[g_ % 2][:, k * 512:(k + 1) * 512]
                h, bq = divmod(k - 8, 4)
                return qrope[g_ % 2][:, 4 * bq:4 * bq + 4, h, :]

            with nc.Block() as blk:
                @blk.sync
                def _(s):
                    phase_wait(s)
                    s.dma_start(cost_sb[:], cost).then_inc(s_c3, 16)
                    s.dma_start(sint_sb[:], sint).then_inc(s_c3, 16)
                    s.dma_start(mtri_sb[:], mtri).then_inc(s_c3, 16)
                    s.dma_start(moth_sb[:], moth).then_inc(s_c3, 16)
                    for g in range(NKV):
                        if g >= 2:
                            s.wait_ge(s_N, (g - 1) * NCH)
                            s.wait_ge(s_vtk[g % 2], 16 * DT * (g // 2))
                        for kt in range(DT):
                            s.dma_start(
                                vtok[g % 2][:, kt, :],
                                v_scr[g * 128:(g + 1) * 128, kt * 128:(kt + 1) * 128],
                                transpose=True,
                            ).then_inc(s_vtk[g % 2], 16)
                    phase_inc(s)

                @blk.gpsimd
                def _(g_):
                    phase_wait(g_)
                    for g in range(NKV):
                        if g >= 1:
                            g_.wait_ge(s_qkraw, g)
                            g_.wait_ge(s_graw, 96 * g)
                        g_.dma_start(kraw[:], k_scr[g * 128:(g + 1) * 128, :]).then_inc(s_graw, 16)
                        g_.dma_start(qraw[:], q_scr_r[:, 4 * g:4 * g + 4, :]).then_inc(s_graw, 16)
                        g_.dma_start(kswap[0:64, :],
                                     k_scr[g * 128 + 64:(g + 1) * 128, :]).then_inc(s_graw, 16)
                        g_.dma_start(kswap[64:128, :],
                                     k_scr[g * 128:g * 128 + 64, :]).then_inc(s_graw, 16)
                        g_.dma_start(qswap[0:64, :, :],
                                     q_scr_r[64:128, 4 * g:4 * g + 4, :]).then_inc(s_graw, 16)
                        g_.dma_start(qswap[64:128, :, :],
                                     q_scr_r[0:64, 4 * g:4 * g + 4, :]).then_inc(s_graw, 16)
                        if g >= 1:
                            g_.wait_ge(s_N, g * NCH)
                            g_.wait_ge(s_og, 64 * (g - 1))
                            for h in range(4):
                                g_.dma_start(
                                    outT_r[:, 4 * (g - 1) + h, :].rearrange(
                                        "p (a b) -> p a b", b=128),
                                    outg[(g - 1) % 2][:, :, h, :],
                                ).then_inc(s_og, 16)
                    g_.wait_ge(s_N, NKV * NCH)
                    g_.wait_ge(s_og, 64 * (NKV - 1))
                    for h in range(4):
                        g_.dma_start(
                            outT_r[:, 4 * (NKV - 1) + h, :].rearrange(
                                "p (a b) -> p a b", b=128),
                            outg[(NKV - 1) % 2][:, :, h, :],
                        ).then_inc(s_og, 16)
                    g_.wait_ge(s_og, 16 * 4 * NKV)
                    phase_inc(g_)

                @blk.tensor
                def _(t):
                    phase_wait(t)
                    t.wait_ge(s_c3, 64)

                    deferred = []
                    avd_pend = []
                    tglob = 0
                    for g in range(NKV):
                        t.wait_ge(s_qkraw, g + 1)
                        t.wait_ge(s_vtk[g % 2], 16 * DT * (g // 2 + 1))
                        for i in range(NCH):
                            Q = g * NCH + i
                            kts = kt_list(i)
                            ntr = len(kts)

                            def mk_avd(pidx, base=tglob, Q=Q, g=g, kts=kts, ntr=ntr):
                                def op():
                                    side, ch = kts[pidx]
                                    vslot = ch if side == "own" else 16 + ch
                                    pfirst = pidx == 0
                                    plast = pidx == ntr - 1
                                    tc = base + pidx
                                    t.wait_ge(s_P, tc // 2 + 1)
                                    if pfirst and Q >= 2:
                                        t.wait_ge(s_N, Q - 1)
                                    t.matmul(ps_av[Q % 2][:], vtok[g % 2][:, vslot, :],
                                             pt[:, tc % 6, :], start=pfirst, stop=plast)
                                    if pidx % 2 == 1 or plast:
                                        for pp in ([pidx - 1, pidx] if pidx % 2 == 1
                                                   else [pidx]):
                                            tc2 = base + pp
                                            mmd = t.matmul(ps_d[Q % 2][:], onec_sb[:],
                                                           pt[:, tc2 % 6, :],
                                                           start=(pp == 0),
                                                           stop=(pp == ntr - 1))
                                            if pp == ntr - 1:
                                                mmd.then_inc(s_qbD, 1)
                                return op

                            for j, (side, ch) in enumerate(kts):
                                tcur = tglob + j
                                masked = (j == i) or (j == ntr - 1)
                                if tcur >= 4:
                                    t.wait_ge(s_P, tcur // 2 - 1)
                                kcols = ch * 128 if side == "own" else 2048 + ch * 128
                                mm = t.matmul(
                                    ps_s[:, (tcur % 4) * 512:(tcur % 4 + 1) * 512],
                                    krope[g % 2][:, kcols:kcols + 128],
                                    qrope[g % 2][:, i, :, :].rearrange("p a b -> p (a b)"),
                                    start=True, stop=(not masked),
                                )
                                if masked:
                                    mk = mtri_sb if j == i else moth_sb
                                    mm = t.matmul(ps_s[:, (tcur % 4) * 512:(tcur % 4 + 1) * 512],
                                                  iden_sb[:], mk[:], start=False, stop=True)
                                mm.then_inc(s_S, 1)
                                if j == 5 and deferred:
                                    deferred.pop(0)()
                                avd_pend.append(mk_avd(j))
                                if len(avd_pend) > 4:
                                    avd_pend.pop(0)()
                            if i <= 2:
                                while avd_pend:
                                    avd_pend.pop(0)()
                            tglob += ntr

                            def rb_op(Q=Q):
                                t.wait_ge(s_avc, Q + 1)
                                t.wait_ge(s_r3, Q + 1)
                                t.matmul(ps_av[Q % 2][:], onerow_bf_sb[:], dr_bf[Q % 2][:],
                                         start=True, stop=True).then_inc(s_rb, 1)
                            deferred.append(rb_op)
                            if i <= 2:
                                deferred.pop(0)()
                        while avd_pend:
                            avd_pend.pop(0)()
                        while deferred:
                            deferred.pop(0)()
                    phase_inc(t)

                @blk.scalar
                def _(sc):
                    phase_wait(sc)
                    tglob = 0
                    pend_av = None
                    for g in range(NKV):
                        for i in range(NCH):
                            Q = g * NCH + i
                            ntr = 2 * (i + 1)
                            for j in range(0, ntr, 2):
                                tcur = tglob + j
                                sc.wait_ge(s_S, tcur + 2)
                                half = (tcur % 4) // 2
                                slot = tcur % 6
                                sc.activation(
                                    pt[:, slot:slot + 2, :].rearrange("p a b -> p (a b)"),
                                    ps_s[:, half * 1024:(half + 1) * 1024],
                                    AF.Exp, bias=negB_sb[:], scale=ISQ,
                                ).then_inc(s_P, 1)
                                if j == 0 and pend_av is not None:
                                    pend_av()
                                    pend_av = None
                            tglob += ntr

                            def av_op(Q=Q):
                                sc.wait_ge(s_qbD, Q + 1)
                                sc.copy(av_sb[Q % 2][:], ps_av[Q % 2][:]).then_inc(s_avc, 1)
                                m1 = sedge(sc.activation(dr_ln[:], ps_d[Q % 2][:], AF.Ln,
                                                         scale=2.0 ** 64))
                                sc.wait_ge(sSelf, m1)
                                sc.activation(dr_bf[Q % 2][:], dr_ln[:], AF.Exp,
                                              scale=-1.0,
                                              bias=cst_sb[0:1, 2:3]).then_inc(s_r3, 1)
                            if i <= 2:
                                av_op()
                            else:
                                pend_av = av_op
                        if pend_av is not None:
                            pend_av()
                            pend_av = None
                    phase_inc(sc)

                @blk.vector
                def _(v):
                    phase_wait(v)
                    v.wait_ge(s_c3, 64)

                    def mk_rope(g_, k):
                        def op():
                            raw, swp, cols = rope_srcs(k)
                            dst = rope_dst(g_, k)
                            v.wait_ge(s_graw, 96 * (g_ + 1))
                            if g_ >= 2:
                                v.wait_ge(s_S, TRI_G * (g_ - 1))
                            n1 = vedge(v.tensor_mul(t1b[k % 2][:], swp, sint_sb[:, cols]))
                            if len(dst.shape) == 3:
                                n2 = vedge(v.tensor_mul(
                                    dst, raw.rearrange("p (a b) -> p a b", b=128),
                                    cost_sb[:, cols].rearrange("p (a b) -> p a b", b=128)))
                                v.wait_ge(vSelf, n2)
                                v.wait_ge(vSelf, n1)
                                mm = v.tensor_add(
                                    dst, dst,
                                    t1b[k % 2][:].rearrange("p (a b) -> p a b", b=128))
                            else:
                                n2 = vedge(v.tensor_mul(dst, raw, cost_sb[:, cols]))
                                v.wait_ge(vSelf, n2)
                                v.wait_ge(vSelf, n1)
                                mm = v.tensor_add(dst, dst, t1b[k % 2][:])
                            if k == 23:
                                mm.then_inc(s_qkraw, 1)
                        return op

                    def v_buckets(g_):
                        bks = [[] for _ in range(NCH)]
                        for k in range(24):
                            bks[min(k // 2 + 1, NCH - 1)].append(mk_rope(g_, k))
                        return bks

                    for op in [o for bk in v_buckets(0) for o in bk]:
                        op()

                    for g in range(NKV):
                        buckets = v_buckets(g + 1) if g + 1 < NKV else [[] for _ in range(NCH)]
                        for i in range(NCH):
                            Q = g * NCH + i
                            if i == 0 and g >= 2:
                                v.wait_ge(s_og, 64 * (g - 1))
                            v.wait_ge(s_rb, Q + 1)
                            v.tensor_mul(
                                outg[g % 2][:, i, :, :].rearrange("p a b -> p (a b)"),
                                av_sb[Q % 2][:], ps_av[Q % 2][:],
                            ).then_inc(s_N, 1)
                            for op in buckets[i]:
                                op()
                    phase_inc(v)
        PH["k"] += 1

        s_or = E(nc.semaphore("s_or"))
        s_w4 = [E(nc.semaphore(f"s_w4{i}")) for i in range(3)]
        s_xt4 = [E(nc.semaphore(f"s_xt4{i}")) for i in range(2)]
        s_mt4 = E(nc.semaphore("s_mt4"))
        s_ev4 = E(nc.semaphore("s_ev4"))
        s_pw4 = [E(nc.semaphore(f"s_pw4{i}")) for i in range(2)]
        with ExitStack() as ph:
            PE = ph.enter_context
            out_res = PE(nc.sbuf_tensor("out_res", [128, DT, NOWN], BF16))
            wocol = [PE(nc.sbuf_tensor(f"wocol{i}", [128, DT, 128], BF16)) for i in range(3)]
            xt_t = [PE(nc.sbuf_tensor(f"xt_t{i}", [128, NOWN], F32)) for i in range(2)]
            stage_o = [PE(nc.sbuf_tensor(f"stage_o{i}", [128, NOWN], F32)) for i in range(2)]
            tmp_u = [PE(nc.sbuf_tensor(f"tmp_u{i}", [128, 512], F32)) for i in range(2)]
            ps_o = [PE(nc.psum_tensor(f"ps_o{i}", [128, 2048], F32)) for i in range(2)]
            with nc.Block() as blk:
                @blk.sync
                def _(s):
                    phase_wait(s)
                    for q4 in range(4):
                        s.dma_start(
                            out_res[:, :, q4 * 512:(q4 + 1) * 512],
                            outT_r[:, :, q4 * 512:(q4 + 1) * 512],
                        ).then_inc(s_or, 16)
                    for mt in range(DT):
                        if mt >= 2:
                            s.wait_ge(s_ev4, mt - 1)
                            s.wait_ge(s_xt4[mt % 2], 16 * (mt // 2))
                        s.dma_start(
                            xt_t[mt % 2][:], xT[mt * 128:(mt + 1) * 128, 0:NOWN]
                        ).then_inc(s_xt4[mt % 2], 16)
                    phase_inc(s)

                @blk.gpsimd
                def _(g):
                    phase_wait(g)
                    for mt in range(DT):
                        if mt >= 3:
                            g.wait_ge(s_mt4, mt - 2)
                            g.wait_ge(s_w4[mt % 3], 16 * (mt // 3))
                        g.dma_start(
                            wocol[mt % 3][:], woT_r[:, :, mt * 128:(mt + 1) * 128]
                        ).then_inc(s_w4[mt % 3], 16)
                    phase_inc(g)

                @blk.tensor
                def _(t):
                    phase_wait(t)
                    t.wait_ge(s_or, 64)
                    for mt in range(DT):
                        t.wait_ge(s_w4[mt % 3], 16 * (mt // 3 + 1))
                        if mt >= 2:
                            t.wait_ge(s_ev4, mt - 1)
                        last = None
                        for dt in range(DT):
                            for nb in range(4):
                                last = t.matmul(
                                    ps_o[mt % 2][:, nb * 512:(nb + 1) * 512],
                                    wocol[mt % 3][:, dt, :],
                                    out_res[:, dt, nb * 512:(nb + 1) * 512],
                                    start=(dt == 0), stop=(dt == DT - 1),
                                )
                        last.then_inc(s_mt4, 1)
                    phase_inc(t)

                @blk.vector
                def _(v):
                    phase_wait(v)
                    for mt in range(DT):
                        v.wait_ge(s_mt4, mt + 1)
                        v.wait_ge(s_xt4[mt % 2], 16 * (mt // 2 + 1))
                        if mt >= 2:
                            v.wait_ge(s_pw4[mt % 2], 16 * (mt // 2))
                        last = None
                        for nb in range(4):
                            cols = slice(nb * 512, (nb + 1) * 512)
                            n1 = vedge(v.tensor_scalar_add(tmp_u[nb % 2][:],
                                                           ps_o[mt % 2][:, cols],
                                                           bsum_sb[:, mt:mt + 1]))
                            v.wait_ge(vSelf, n1)
                            last = v.tensor_add(stage_o[mt % 2][:, cols], tmp_u[nb % 2][:],
                                                xt_t[mt % 2][:, cols])
                        last.then_inc(s_ev4, 1)
                    phase_inc(v)

                @blk.scalar
                def _(sc):
                    phase_wait(sc)
                    for mt in range(DT):
                        sc.wait_ge(s_ev4, mt + 1)
                        if mt >= 2:
                            sc.wait_ge(s_pw4[mt % 2], 16 * (mt // 2))
                        sc.dma_start(
                            projT_scr[mt * 128:(mt + 1) * 128, :], stage_o[mt % 2][:]
                        ).then_inc(s_pw4[mt % 2], 16)
                    sc.wait_ge(s_pw4[0], 16 * (DT // 2))
                    sc.wait_ge(s_pw4[1], 16 * (DT // 2))
                    phase_inc(sc)
        PH["k"] += 1

        s_xn5 = [E(nc.semaphore(f"s_xn5{i}")) for i in range(2)]
        s_gw5 = E(nc.semaphore("s_gw5"))
        s_lg = E(nc.semaphore("s_lg"))
        s_vm = E(nc.semaphore("s_vm"))
        s_e8 = E(nc.semaphore("s_e8"))
        s_rt = E(nc.semaphore("s_rt"))
        s_tr = E(nc.semaphore("s_tr"))
        s_rc = E(nc.semaphore("s_rc"))
        s_sl5 = [E(nc.semaphore(f"s_sl5{i}")) for i in range(2)]
        s_im = E(nc.semaphore("s_im"))
        s_is5 = E(nc.semaphore("s_is5"))
        s_xw5 = [E(nc.semaphore(f"s_xw5{i}")) for i in range(2)]
        with ExitStack() as ph:
            PE = ph.enter_context
            gw_sb = PE(nc.sbuf_tensor("gw_sb", [128, DT, 8], BF16))
            ics_sb = PE(nc.sbuf_tensor("ics_sb", [8, D], BF16))
            xn_t = [PE(nc.sbuf_tensor(f"xn_t{i}", [128, DT, 128], BF16)) for i in range(2)]
            e8 = PE(nc.sbuf_tensor("e8", [128, 8], F32))
            negmx = PE(nc.sbuf_tensor("negmx", [128, 1], F32))
            sm8 = PE(nc.sbuf_tensor("sm8", [128, 1], F32))
            rs8 = PE(nc.sbuf_tensor("rs8", [128, 1], F32))
            rout_bf = PE(nc.sbuf_tensor("rout_bf", [128, 8], BF16))
            xn_sl = [PE(nc.sbuf_tensor(f"xn_sl{i}", [128, DT, 512], BF16)) for i in range(2)]
            xs_sl = [PE(nc.sbuf_tensor(f"xs_sl{i}", [128, DT, 512], BF16)) for i in range(2)]
            ps_l = PE(nc.psum_tensor("ps_l", [128, 8], F32))
            ps_rt = PE(nc.psum_tensor("ps_rt", [8, 128], BF16))
            ps_i = [PE(nc.psum_tensor(f"ps_i{i}", [128, 512], F32)) for i in range(2)]
            with nc.Block() as blk:
                @blk.sync
                def _(s):
                    phase_wait(s)
                    for nt in range(NCH):
                        if nt >= 2:
                            s.wait_ge(s_lg, nt - 1)
                            s.wait_ge(s_xn5[nt % 2], 16 * (nt // 2))
                        s.dma_start(
                            xn_t[nt % 2][:], xnT_r[:, :, nt * 128:(nt + 1) * 128]
                        ).then_inc(s_xn5[nt % 2], 16)
                    for nb in range(4):
                        if nb >= 2:
                            s.wait_ge(s_is5, DT * (nb - 1))
                            s.wait_ge(s_sl5[nb % 2], 16 * (nb // 2))
                        s.dma_start(
                            xn_sl[nb % 2][:], xnT_r[:, :, nb * 512:(nb + 1) * 512]
                        ).then_inc(s_sl5[nb % 2], 16)
                    phase_inc(s)

                @blk.gpsimd
                def _(g):
                    phase_wait(g)
                    g.dma_start(gw_sb[:], gwT_r[:]).then_inc(s_gw5, 16)
                    g.dma_start(ics_sb[:], ics8).then_inc(s_gw5, 16)
                    for nb in range(4):
                        g.wait_ge(s_is5, DT * (nb + 1))
                        if nb >= 2:
                            g.wait_ge(s_xw5[nb % 2], 16 * (nb // 2))
                        g.dma_start(
                            xsT_r[:, :, nb * 512:(nb + 1) * 512], xs_sl[nb % 2][:]
                        ).then_inc(s_xw5[nb % 2], 16)
                    g.wait_ge(s_xw5[0], 32)
                    g.wait_ge(s_xw5[1], 32)
                    phase_inc(g)

                @blk.tensor
                def _(t):
                    phase_wait(t)
                    t.wait_ge(s_gw5, 32)
                    for nt in range(NCH):
                        t.wait_ge(s_xn5[nt % 2], 16 * (nt // 2 + 1))
                        if nt >= 1:
                            t.wait_ge(s_e8, nt)
                        last = None
                        for dt in range(DT):
                            last = t.matmul(ps_l[:], xn_t[nt % 2][:, dt, :], gw_sb[:, dt, :],
                                            start=(dt == 0), stop=(dt == DT - 1))
                        last.then_inc(s_lg, 1)
                        t.wait_ge(s_rt, nt + 1)
                        if nt >= 1:
                            t.wait_ge(s_rc, nt)
                        t.transpose(ps_rt[:], rout_bf[:], iden_sb[:]).then_inc(s_tr, 1)
                    t.wait_ge(s_rc, NCH)
                    for nb in range(4):
                        t.wait_ge(s_sl5[nb % 2], 16 * (nb // 2 + 1))
                        for dt in range(DT):
                            k = nb * DT + dt
                            if k >= 2:
                                t.wait_ge(s_is5, k - 1)
                            t.matmul(ps_i[k % 2][:], ics_sb[:, dt * 128:(dt + 1) * 128],
                                     routT_sb[:, nb * 512:(nb + 1) * 512],
                                     start=True, stop=True).then_inc(s_im, 1)
                    phase_inc(t)

                @blk.vector
                def _(v):
                    phase_wait(v)
                    for nt in range(NCH):
                        v.wait_ge(s_lg, nt + 1)
                        if nt >= 1:
                            v.wait_ge(s_e8, nt)
                        v.reduce_max(negmx[:], ps_l[:], axis=AX, negate=True).then_inc(s_vm, 1)
                        v.wait_ge(s_e8, nt + 1)
                        if nt >= 1:
                            v.wait_ge(s_tr, nt)
                        n1 = vedge(v.reduce_sum(sm8[:], e8[:], axis=AX))
                        v.wait_ge(vSelf, n1)
                        n2 = vedge(v.reciprocal(rs8[:], sm8[:]))
                        v.wait_ge(vSelf, n2)
                        v.tensor_scalar_mul(rout_bf[:], e8[:], rs8[:]).then_inc(s_rt, 1)
                    for nb in range(4):
                        for dt in range(DT):
                            k = nb * DT + dt
                            v.wait_ge(s_im, k + 1)
                            if nb >= 2 and dt == 0:
                                v.wait_ge(s_xw5[nb % 2], 16 * (nb // 2))
                            v.tensor_mul(xs_sl[nb % 2][:, dt, :], xn_sl[nb % 2][:, dt, :],
                                         ps_i[k % 2][:]).then_inc(s_is5, 1)
                    phase_inc(v)

                @blk.scalar
                def _(sc):
                    phase_wait(sc)
                    for nt in range(NCH):
                        sc.wait_ge(s_vm, nt + 1)
                        if nt >= 1:
                            sc.wait_ge(s_rt, nt)
                        sc.activation(e8[:], ps_l[:], AF.Exp, bias=negmx[:],
                                      scale=1.0).then_inc(s_e8, 1)
                        sc.wait_ge(s_tr, nt + 1)
                        sc.copy(routT_sb[:, nt * 128:(nt + 1) * 128], ps_rt[:]).then_inc(s_rc, 1)
                    phase_inc(sc)
        PH["k"] += 1

        s_xs6 = E(nc.semaphore("s_xs6"))
        s_w6 = E(nc.semaphore("s_w6"))
        s_sg6 = E(nc.semaphore("s_sg6"))
        s_os6 = E(nc.semaphore("s_os6"))
        s_oc6 = E(nc.semaphore("s_oc6"))
        s_jt6 = E(nc.semaphore("s_jt6"))
        s_fin6 = E(nc.semaphore("s_fin6"))
        s_pj6 = [E(nc.semaphore(f"s_pj6{i}")) for i in range(2)]
        s_out = [E(nc.semaphore(f"s_out{i}")) for i in range(2)]
        with ExitStack() as ph:
            PE = ph.enter_context
            xs_res = PE(nc.sbuf_tensor("xs_res", [128, DT, NOWN], BF16))
            wcol_f = PE(nc.sbuf_tensor("wcol_f", [128, 16, 128], F32))
            wsign = [PE(nc.sbuf_tensor(f"wsign{i}", [128, DT, 128], BF16)) for i in range(2)]
            os_sb = [PE(nc.sbuf_tensor(f"os_sb{i}", [128, 2048], BF16)) for i in range(2)]
            proj_t = [PE(nc.sbuf_tensor(f"proj_t{i}", [128, NOWN], F32)) for i in range(2)]
            out_st = [PE(nc.sbuf_tensor(f"out_st{i}", [128, NOWN], F32)) for i in range(2)]
            ps_f = [PE(nc.psum_tensor(f"ps_f{i}", [128, 2048], F32)) for i in range(2)]
            with nc.Block() as blk:
                @blk.sync
                def _(s):
                    phase_wait(s)
                    for q4 in range(4):
                        s.dma_start(
                            xs_res[:, :, q4 * 512:(q4 + 1) * 512],
                            xsT_r[:, :, q4 * 512:(q4 + 1) * 512],
                        ).then_inc(s_xs6, 16)
                    for jt in range(DT):
                        if jt >= 2:
                            s.wait_ge(s_fin6, 4 * (jt - 1))
                            s.wait_ge(s_pj6[jt % 2], 16 * (jt // 2))
                        s.dma_start(
                            proj_t[jt % 2][:], projT_scr[jt * 128:(jt + 1) * 128, :]
                        ).then_inc(s_pj6[jt % 2], 16)
                    s.wait_ge(s_out[0], 16 * (DT // 2))
                    s.wait_ge(s_out[1], 16 * (DT // 2))
                    phase_inc(s)

                @blk.gpsimd
                def _(g):
                    phase_wait(g)
                    for jt in range(DT):
                        for hf in range(2):
                            k = 2 * jt + hf
                            if k >= 1:
                                g.wait_ge(s_sg6, k)
                                g.wait_ge(s_w6, 16 * k)
                            g.dma_start(
                                wcol_f[:],
                                wfT_r[:, 16 * hf:16 * (hf + 1), jt * 128:(jt + 1) * 128],
                            ).then_inc(s_w6, 16)
                    phase_inc(g)

                @blk.scalar
                def _(sc):
                    phase_wait(sc)
                    for jt in range(DT):
                        for hf in range(2):
                            sc.wait_ge(s_w6, 16 * (2 * jt + hf + 1))
                            if jt >= 2 and hf == 0:
                                sc.wait_ge(s_jt6, jt - 1)
                            sc.activation(wsign[jt % 2][:, 16 * hf:16 * (hf + 1), :],
                                          wcol_f[:], AF.Sign).then_inc(s_sg6, 1)
                        for nb in range(4):
                            sc.wait_ge(s_os6, 4 * jt + nb + 1)
                            sc.copy(os_sb[jt % 2][:, nb * 512:(nb + 1) * 512],
                                    ps_f[jt % 2][:, 0:512]).then_inc(s_oc6, 1)
                        if jt >= 1:
                            sc.wait_ge(s_fin6, 4 * jt)
                            if jt >= 3:
                                sc.wait_ge(s_out[(jt - 1) % 2], 16 * ((jt - 1) // 2))
                            sc.dma_start(
                                yT[(jt - 1) * 128:jt * 128, :], out_st[(jt - 1) % 2][:]
                            ).then_inc(s_out[(jt - 1) % 2], 16)
                    sc.wait_ge(s_fin6, 4 * DT)
                    sc.wait_ge(s_out[(DT - 1) % 2], 16 * ((DT - 1) // 2))
                    sc.dma_start(
                        yT[(DT - 1) * 128:DT * 128, :], out_st[(DT - 1) % 2][:]
                    ).then_inc(s_out[(DT - 1) % 2], 16)
                    phase_inc(sc)

                @blk.tensor
                def _(t):
                    phase_wait(t)
                    t.wait_ge(s_xs6, 64)
                    for jt in range(DT):
                        if jt >= 2:
                            t.wait_ge(s_fin6, 4 * (jt - 1))
                        for nb in range(4):
                            if 4 * jt + nb >= 1:
                                t.wait_ge(s_oc6, 4 * jt + nb)
                            t.matmul(ps_f[jt % 2][:, 0:512],
                                     ocs_sb[0:8, jt * 128:(jt + 1) * 128],
                                     routT_sb[:, nb * 512:(nb + 1) * 512],
                                     start=True, stop=True).then_inc(s_os6, 1)
                        t.wait_ge(s_oc6, 4 * (jt + 1))
                        t.wait_ge(s_sg6, 2 * (jt + 1))
                        last = None
                        for dt in range(DT):
                            for nb in range(4):
                                last = t.matmul(
                                    ps_f[jt % 2][:, nb * 512:(nb + 1) * 512],
                                    wsign[jt % 2][:, dt, :],
                                    xs_res[:, dt, nb * 512:(nb + 1) * 512],
                                    start=(dt == 0), stop=(dt == DT - 1),
                                )
                        last.then_inc(s_jt6, 1)
                    phase_inc(t)

                @blk.vector
                def _(v):
                    phase_wait(v)
                    for jt in range(DT):
                        v.wait_ge(s_jt6, jt + 1)
                        v.wait_ge(s_pj6[jt % 2], 16 * (jt // 2 + 1))
                        if jt >= 2:
                            v.wait_ge(s_out[jt % 2], 16 * (jt // 2))
                        for nb in range(4):
                            cols = slice(nb * 512, (nb + 1) * 512)
                            n1 = vedge(v.tensor_mul(out_st[jt % 2][:, cols],
                                                    ps_f[jt % 2][:, cols],
                                                    os_sb[jt % 2][:, cols]))
                            v.wait_ge(vSelf, n1)
                            v.tensor_add(out_st[jt % 2][:, cols], out_st[jt % 2][:, cols],
                                         proj_t[jt % 2][:, cols]).then_inc(s_fin6, 1)
                    phase_inc(v)
        PH["k"] += 1

    return nc


def _host_prep(
    L0_attn_x, L0_attn_w_qkv, L0_attn_b_qkv, L0_attn_w_o, L0_attn_b_o,
    L0_ffn_weight, L0_ffn_bias, L0_ffn_gate_w, L0_ffn_in_channel_scale,
    L0_ffn_out_channel_scale,
):
    x = np.asarray(L0_attn_x, np.float32)
    wq = np.asarray(L0_attn_w_qkv, np.float32)
    bq = np.asarray(L0_attn_b_qkv, np.float32)
    wo = np.asarray(L0_attn_w_o, np.float32)
    bo_ = np.asarray(L0_attn_b_o, np.float32)
    wf = np.asarray(L0_ffn_weight, np.float32)
    bf_ = np.asarray(L0_ffn_bias, np.float32)
    gw = np.asarray(L0_ffn_gate_w, np.float32)
    ics = np.asarray(L0_ffn_in_channel_scale, np.float32)
    ocs = np.asarray(L0_ffn_out_channel_scale, np.float32)

    sigma = np.concatenate([np.arange(0, HD, 2), np.arange(1, HD, 2)])
    qk_perm = np.arange(6144)
    for hh in range(40):
        qk_perm[hh * HD:(hh + 1) * HD] = hh * HD + sigma
    wqT = np.ascontiguousarray(wq.T[:, qk_perm])
    woT = np.ascontiguousarray(wo.T)
    wfT = np.ascontiguousarray(wf.T)
    gwT = np.ascontiguousarray(gw.T)
    bqkv_t = np.ascontiguousarray(bq[qk_perm].reshape(JW, 128).T)
    bo_t = np.ascontiguousarray(bo_.reshape(DT, 128).T)
    bf_t = np.ascontiguousarray(bf_.reshape(DT, 128).T)

    iden_m = np.eye(128, dtype=ml_dtypes.bfloat16)
    perm_m = np.zeros((128, 128), dtype=ml_dtypes.bfloat16)
    for i in range(64):
        perm_m[2 * i, 2 * i + 1] = 1
        perm_m[2 * i + 1, 2 * i] = 1
    onec_m = np.ones((128, 1), dtype=ml_dtypes.bfloat16)
    onerow_m = np.ones((1, 128), dtype=np.float32)
    onerow_bf_m = np.ones((1, 128), dtype=ml_dtypes.bfloat16)
    tri = np.where(np.arange(128)[:, None] <= np.arange(128)[None, :], 0.0, MBIG)
    mtri_m = np.tile(tri, (1, 4)).astype(ml_dtypes.bfloat16)

    idx = np.arange(S).reshape(32, 128)
    inv_freq = 1.0 / (10000.0 ** (np.arange(0, HD, 2, dtype=np.float64) / HD))

    in_maps = []
    for c in range(8):
        b, p = divmod(c, 2)
        order = np.concatenate([idx[p::2].ravel(), idx[1 - p::2].ravel()])
        xT_store = np.ascontiguousarray(x[b].T[:, order])
        ang = order[None, :].astype(np.float64) * inv_freq[:, None]
        cos_t = np.empty((128, S), np.float64)
        sin_t = np.empty((128, S), np.float64)
        cos_t[0:64] = np.cos(ang)
        cos_t[64:128] = np.cos(ang)
        sin_t[0:64] = -np.sin(ang)
        sin_t[64:128] = np.sin(ang)
        moth_m = np.full((128, 512), MBIG if p == 0 else 0.0, dtype=ml_dtypes.bfloat16)
        in_maps.append({
            "xT": xT_store, "wqT": wqT, "woT": woT, "wfT": wfT, "gwT": gwT,
            "ics8": ics, "ocs8": ocs, "bqkv": bqkv_t, "bo": bo_t, "bff": bf_t,
            "cost": cos_t.astype(np.float32), "sint": sin_t.astype(np.float32),
            "mtri": mtri_m, "moth": moth_m, "perm": perm_m, "iden": iden_m,
            "onec": onec_m, "onerow": onerow_m, "onerow_bf": onerow_bf_m,
        })
    return in_maps


def kernel(**inputs):
    in_maps = _host_prep(**inputs)
    nc = _build()
    res = run_bass_kernel_spmd(nc, in_maps, core_ids=list(range(8)))
    idx = np.arange(S).reshape(32, 128)
    out = np.empty((B, S, D), np.float32)
    for c in range(8):
        b, p = divmod(c, 2)
        own = idx[p::2].ravel()
        out[b][own, :] = res.results[c]["yT"].T
    return out


# version 08
# speedup vs baseline: 1.0894x; 1.0894x over previous
import math
import numpy as np
import ml_dtypes
from contextlib import ExitStack

import concourse.bass as bass
import concourse.mybir as mybir
from concourse.bass import broadcast_tensor_aps
from concourse.bass_utils import run_bass_kernel_spmd

F32 = mybir.dt.float32
BF16 = mybir.dt.bfloat16
AF = mybir.ActivationFunctionType
AX = mybir.AxisListType.X

D = 4096
S = 4096
B = 4
HD = 128
NKV = 8
NOWN = 2048
NCH = 16
DT = 32
JW = 48
ISQ = 1.0 / math.sqrt(HD)
MBIG = -30000.0
EPS = 1e-5


def _build(dbg=frozenset()):
    nc = bass.Bass("TRN2", target_bir_lowering=False)

    def dram(name, shape, dt, kind):
        if kind == "Internal" and name in dbg:
            kind = "ExternalOutput"
        return nc.dram_tensor(name, shape, dt, kind=kind).ap()

    xT = dram("xT", [D, S], F32, "ExternalInput")
    wqT = dram("wqT", [D, 6144], F32, "ExternalInput")
    woT = dram("woT", [D, D], F32, "ExternalInput")
    wfT = dram("wfT", [D, D], F32, "ExternalInput")
    gwT = dram("gwT", [D, 8], F32, "ExternalInput")
    ics8 = dram("ics8", [8, D], F32, "ExternalInput")
    ocs8 = dram("ocs8", [8, D], F32, "ExternalInput")
    bqkv = dram("bqkv", [128, JW], F32, "ExternalInput")
    bo = dram("bo", [128, DT], F32, "ExternalInput")
    bff = dram("bff", [128, DT], F32, "ExternalInput")
    cost = dram("cost", [128, S], F32, "ExternalInput")
    sint = dram("sint", [128, S], F32, "ExternalInput")
    mtri = dram("mtri", [128, 512], BF16, "ExternalInput")
    moth = dram("moth", [128, 512], BF16, "ExternalInput")
    perm = dram("perm", [128, 128], BF16, "ExternalInput")
    iden = dram("iden", [128, 128], BF16, "ExternalInput")
    onec = dram("onec", [128, 1], BF16, "ExternalInput")
    onerow = dram("onerow", [1, 128], F32, "ExternalInput")
    onerow_bf = dram("onerow_bf", [1, 128], BF16, "ExternalInput")
    yT = dram("yT", [D, NOWN], F32, "ExternalOutput")

    xnT_scr = dram("xnT_scr", [D, S], BF16, "Internal")
    q_scr = dram("q_scr", [D, NOWN], BF16, "Internal")
    k_scr = dram("k_scr", [NKV * HD, S], BF16, "Internal")
    v_scr = dram("v_scr", [NKV * HD, S], BF16, "Internal")
    outT_scr = dram("outT_scr", [D, NOWN], BF16, "Internal")
    projT_scr = dram("projT_scr", [D, NOWN], F32, "Internal")
    xsT_scr = dram("xsT_scr", [D, NOWN], BF16, "Internal")
    v_tok_scr = dram("v_tok_scr", [NKV * S, HD], BF16, "Internal")

    xnT_r = xnT_scr.rearrange("(dt part) s -> part dt s", part=128)
    xT_r = xT.rearrange("(dt part) s -> part dt s", part=128)
    wqT_r = wqT.rearrange("(dt part) j -> part dt j", part=128)
    woT_r = woT.rearrange("(dt part) j -> part dt j", part=128)
    wfT_r = wfT.rearrange("(dt part) j -> part dt j", part=128)
    gwT_r = gwT.rearrange("(dt part) j -> part dt j", part=128)
    q_scr_r = q_scr.rearrange("(h part) s -> part h s", part=128)
    outT_r = outT_scr.rearrange("(dt part) s -> part dt s", part=128)
    xsT_r = xsT_scr.rearrange("(dt part) s -> part dt s", part=128)
    vtok_r = v_tok_scr.rearrange("(g kt p) d -> g p kt d", p=128, kt=DT)

    with ExitStack() as top:
        E = top.enter_context
        bqkv_sb = E(nc.sbuf_tensor("bqkv_sb", [128, JW], F32))
        bo_sb = E(nc.sbuf_tensor("bo_sb", [128, DT], F32))
        bff_sb = E(nc.sbuf_tensor("bff_sb", [128, DT], F32))
        bsum_sb = E(nc.sbuf_tensor("bsum_sb", [128, DT], F32))
        iden_sb = E(nc.sbuf_tensor("iden_sb", [128, 128], BF16))
        onec_sb = E(nc.sbuf_tensor("onec_sb", [128, 1], BF16))
        onerow_sb = E(nc.sbuf_tensor("onerow_sb", [1, 128], F32))
        onerow_bf_sb = E(nc.sbuf_tensor("onerow_bf_sb", [1, 128], BF16))
        ocs_sb = E(nc.sbuf_tensor("ocs_sb", [8, D], BF16))
        routT_sb = E(nc.sbuf_tensor("routT_sb", [8, NOWN], BF16))
        negB_sb = E(nc.sbuf_tensor("negB_sb", [128, 1], F32))
        cst_sb = E(nc.sbuf_tensor("cst_sb", [1, 8], F32))

        sB = E(nc.semaphore("sB"))
        vSelf = E(nc.semaphore("vSelf"))
        sSelf = E(nc.semaphore("sSelf"))
        SELF = {"v": 0, "s": 0}

        def vedge(instr):
            SELF["v"] += 1
            instr.then_inc(vSelf, 1)
            return SELF["v"]

        def sedge(instr):
            SELF["s"] += 1
            instr.then_inc(sSelf, 1)
            return SELF["s"]

        PH = {"k": 0}

        def phase_wait(e):
            e.wait_ge(sB, 5 * PH["k"])

        def phase_inc(e):
            e.sem_inc(sB, 1)

        s_g = E(nc.semaphore("s_g"))
        s_g2 = E(nc.semaphore("s_g2"))
        with nc.Block() as blk:
            @blk.sync
            def _(s):
                for dst, src in [
                    (bqkv_sb, bqkv), (bo_sb, bo), (bff_sb, bff),
                    (iden_sb, iden), (onec_sb, onec), (onerow_sb, onerow),
                    (onerow_bf_sb, onerow_bf),
                ]:
                    s.dma_start(dst[:], src).then_inc(s_g, 16)
                phase_inc(s)

            @blk.gpsimd
            def _(g):
                g.dma_start(ocs_sb[:], ocs8).then_inc(s_g2, 16)
                phase_inc(g)

            @blk.vector
            def _(v):
                v.wait_ge(s_g, 112)
                v.wait_ge(s_g2, 16)
                v.tensor_add(bsum_sb[:], bo_sb[:], bff_sb[:])
                v.memset(cst_sb[0:1, 0:1], EPS)
                v.memset(cst_sb[0:1, 1:2], math.log(ISQ))
                v.memset(cst_sb[0:1, 2:3], 64.0 * math.log(2.0))
                phase_inc(v)

            @blk.tensor
            def _(t):
                phase_inc(t)

            @blk.scalar
            def _(sc):
                phase_inc(sc)
        PH["k"] += 1

        NG = 16
        GSZ = 256
        s_x = [E(nc.semaphore(f"s_x{i}")) for i in range(2)]
        s_v1 = E(nc.semaphore("s_v1"))
        s_sq1 = E(nc.semaphore("s_sq1"))
        s_t1 = E(nc.semaphore("s_t1"))
        s_v2 = E(nc.semaphore("s_v2"))
        s_v3 = E(nc.semaphore("s_v3"))
        s_sc0 = E(nc.semaphore("s_sc0"))
        s_v4 = E(nc.semaphore("s_v4"))
        s_t2 = E(nc.semaphore("s_t2"))
        s_sc1 = E(nc.semaphore("s_sc1"))
        s_v5 = E(nc.semaphore("s_v5"))
        s_sp = [E(nc.semaphore(f"s_sp{i}")) for i in range(2)]
        with ExitStack() as ph:
            PE = ph.enter_context
            xg = [PE(nc.sbuf_tensor(f"xg{i}", [128, DT, GSZ], F32)) for i in range(2)]
            xb = [PE(nc.sbuf_tensor(f"xb{i}", [128, DT, GSZ], BF16)) for i in range(2)]
            sq = [PE(nc.sbuf_tensor(f"sq{i}", [128, DT, GSZ], BF16)) for i in range(2)]
            xn = [PE(nc.sbuf_tensor(f"xn{i}", [128, DT, GSZ], BF16)) for i in range(2)]
            sb_s = [PE(nc.sbuf_tensor(f"sb_s{i}", [128, 1, GSZ], BF16)) for i in range(2)]
            sb_ms = [PE(nc.sbuf_tensor(f"sb_ms{i}", [128, 1, GSZ], BF16)) for i in range(2)]
            m_row = PE(nc.sbuf_tensor("m_row", [1, GSZ], F32))
            ex2_row = PE(nc.sbuf_tensor("ex2_row", [1, GSZ], F32))
            m2_row = PE(nc.sbuf_tensor("m2_row", [1, GSZ], F32))
            var_row = PE(nc.sbuf_tensor("var_row", [1, GSZ], F32))
            ln_row = PE(nc.sbuf_tensor("ln_row", [1, GSZ], F32))
            rstd_row = PE(nc.sbuf_tensor("rstd_row", [1, GSZ], F32))
            ms_row = PE(nc.sbuf_tensor("ms_row", [1, GSZ], F32))
            ps_sx = PE(nc.psum_tensor("ps_sx", [1, GSZ], F32))
            ps_sx2 = PE(nc.psum_tensor("ps_sx2", [1, GSZ], F32))
            ps_b1 = PE(nc.psum_tensor("ps_b1", [128, GSZ], F32))
            ps_b2 = PE(nc.psum_tensor("ps_b2", [128, GSZ], F32))
            with nc.Block() as blk:
                @blk.sync
                def _(s):
                    phase_wait(s)
                    for g in range(NG):
                        if g >= 2:
                            s.wait_ge(s_v1, g - 1)
                            s.wait_ge(s_sq1, g - 1)
                            s.wait_ge(s_x[g % 2], 16 * (g // 2))
                        s.dma_start(
                            xg[g % 2][:], xT_r[:, :, g * GSZ:(g + 1) * GSZ]
                        ).then_inc(s_x[g % 2], 16)
                    phase_inc(s)

                @blk.vector
                def _(v):
                    phase_wait(v)
                    for g in range(NG + 1):
                        if g < NG:
                            v.wait_ge(s_x[g % 2], 16 * (g // 2 + 1))
                            if g >= 2:
                                v.wait_ge(s_t1, g - 1)
                            v.tensor_copy(xb[g % 2][:], xg[g % 2][:]).then_inc(s_v1, 1)
                        if g >= 1:
                            p = g - 1
                            v.wait_ge(s_t1, p + 1)
                            na = vedge(v.tensor_scalar_mul(m_row[:], ps_sx[:], 1.0 / D))
                            v.tensor_scalar_mul(ex2_row[:], ps_sx2[:],
                                                1.0 / D).then_inc(s_v2, 1)
                            v.wait_ge(vSelf, na)
                            n1 = vedge(v.tensor_mul(m2_row[:], m_row[:], m_row[:]))
                            v.wait_ge(s_v2, p + 1)
                            v.wait_ge(vSelf, n1)
                            v.tensor_sub(var_row[:], ex2_row[:], m2_row[:]).then_inc(s_v3, 1)
                            v.wait_ge(s_sc0, p + 1)
                            v.tensor_mul(ms_row[:], m_row[:], rstd_row[:]).then_inc(s_v4, 1)
                            v.wait_ge(s_sc1, p + 1)
                            if p >= 2:
                                v.wait_ge(s_sp[p % 2], 16 * (p // 2))
                            a1, b1 = broadcast_tensor_aps(xb[p % 2][:], sb_s[p % 2][:])
                            n2 = vedge(v.tensor_mul(xn[p % 2][:], a1, b1))
                            v.wait_ge(vSelf, n2)
                            a2, b2 = broadcast_tensor_aps(xn[p % 2][:], sb_ms[p % 2][:])
                            v.tensor_sub(xn[p % 2][:], a2, b2).then_inc(s_v5, 1)
                    phase_inc(v)

                @blk.scalar
                def _(sc):
                    phase_wait(sc)
                    for g in range(NG + 1):
                        if g < NG:
                            sc.wait_ge(s_x[g % 2], 16 * (g // 2 + 1))
                            if g >= 2:
                                sc.wait_ge(s_t1, g - 1)
                            sc.activation(sq[g % 2][:], xg[g % 2][:],
                                          AF.Square).then_inc(s_sq1, 1)
                        if g >= 1:
                            p = g - 1
                            sc.wait_ge(s_v3, p + 1)
                            if p >= 1:
                                sc.wait_ge(s_t2, p)
                            m1 = sedge(sc.activation(ln_row[:], var_row[:], AF.Ln,
                                                     bias=cst_sb[0:1, 0:1]))
                            sc.wait_ge(sSelf, m1)
                            sc.activation(rstd_row[:], ln_row[:], AF.Exp,
                                          scale=-0.5).then_inc(s_sc0, 1)
                            sc.wait_ge(s_t2, p + 1)
                            sc.copy(sb_s[p % 2][:, 0, :], ps_b1[:])
                            sc.copy(sb_ms[p % 2][:, 0, :], ps_b2[:]).then_inc(s_sc1, 1)
                    phase_inc(sc)

                @blk.tensor
                def _(t):
                    phase_wait(t)
                    for g in range(NG + 1):
                        if g < NG:
                            t.wait_ge(s_v1, g + 1)
                            t.wait_ge(s_sq1, g + 1)
                            if g >= 1:
                                t.wait_ge(s_v2, g)
                            for dt in range(DT):
                                t.matmul(ps_sx[:], onec_sb[:], xb[g % 2][:, dt, :],
                                         start=(dt == 0), stop=(dt == DT - 1))
                            last = None
                            for dt in range(DT):
                                last = t.matmul(ps_sx2[:], onec_sb[:], sq[g % 2][:, dt, :],
                                                start=(dt == 0), stop=(dt == DT - 1))
                            last.then_inc(s_t1, 1)
                        if g >= 1:
                            p = g - 1
                            t.wait_ge(s_v4, p + 1)
                            if p >= 1:
                                t.wait_ge(s_sc1, p)
                            t.matmul(ps_b1[:], onerow_sb[:], rstd_row[:], start=True, stop=True)
                            t.matmul(ps_b2[:], onerow_sb[:], ms_row[:],
                                     start=True, stop=True).then_inc(s_t2, 1)
                    phase_inc(t)

                @blk.gpsimd
                def _(g_):
                    phase_wait(g_)
                    for g in range(NG):
                        g_.wait_ge(s_v5, g + 1)
                        if g >= 2:
                            g_.wait_ge(s_sp[g % 2], 16 * (g // 2))
                        g_.dma_start(
                            xnT_r[:, :, g * GSZ:(g + 1) * GSZ], xn[g % 2][:]
                        ).then_inc(s_sp[g % 2], 16)
                    g_.wait_ge(s_sp[0], 16 * (NG // 2))
                    g_.wait_ge(s_sp[1], 16 * (NG // 2))
                    phase_inc(g_)
        PH["k"] += 1

        jobs = [(w, 0) for w in range(JW)] + [(w, 1) for w in range(32, JW)]
        bd_jobs = [i for i, (w, h) in enumerate(jobs) if w < 40]
        bd_rank = {j: r for r, j in enumerate(bd_jobs)}
        first_k_rank = min(r for r, j in enumerate(bd_jobs) if jobs[j][0] >= 32)
        s_xh = E(nc.semaphore("s_xh"))
        s_xh2 = E(nc.semaphore("s_xh2"))
        s_w2 = [E(nc.semaphore(f"s_w2{i}")) for i in range(3)]
        s_jt2 = E(nc.semaphore("s_jt2"))
        s_ev2 = E(nc.semaphore("s_ev2"))
        s_st2 = [E(nc.semaphore(f"s_st2{i}")) for i in range(2)]
        s_sqst = E(nc.semaphore("s_sqst"))
        s_bnd4 = E(nc.semaphore("s_bnd4"))
        s_bdr = E(nc.semaphore("s_bdr"))
        s_nbB = E(nc.semaphore("s_nbB"))
        with ExitStack() as ph:
            PE = ph.enter_context
            xnh = PE(nc.sbuf_tensor("xnh", [128, DT, NOWN], BF16))
            wcol = [PE(nc.sbuf_tensor(f"wcol{i}", [128, DT, 128], BF16)) for i in range(3)]
            stage = [PE(nc.sbuf_tensor(f"stage{i}", [128, NOWN], BF16)) for i in range(2)]
            sqst = PE(nc.sbuf_tensor("sqst", [128, NOWN], BF16))
            qmax_r = PE(nc.sbuf_tensor("qmax_r", [1, 1], F32))
            kmax_r = PE(nc.sbuf_tensor("kmax_r", [1, 1], F32))
            blkr = PE(nc.sbuf_tensor("blkr", [1, 1], F32))
            bbr = PE(nc.sbuf_tensor("bbr", [1, 1], F32))
            ps_q = [PE(nc.psum_tensor(f"ps_q{i}", [128, 2048], F32)) for i in range(2)]

            def p2_dest(w, half):
                cols = slice(half * 2048, (half + 1) * 2048)
                if w < 32:
                    return q_scr[w * 128:(w + 1) * 128, :]
                if w < 40:
                    return k_scr[(w - 32) * 128:(w - 31) * 128, cols]
                return v_scr[(w - 40) * 128:(w - 39) * 128, cols]

            with nc.Block() as blk:
                @blk.sync
                def _(s):
                    phase_wait(s)
                    for q4 in range(4):
                        s.dma_start(
                            xnh[:, :, q4 * 512:(q4 + 1) * 512],
                            xnT_r[:, :, q4 * 512:(q4 + 1) * 512],
                        ).then_inc(s_xh, 16)
                    s.wait_ge(s_jt2, JW)
                    s.wait_ge(s_xh, 64)
                    for q4 in range(4):
                        s.dma_start(
                            xnh[:, :, q4 * 512:(q4 + 1) * 512],
                            xnT_r[:, :, 2048 + q4 * 512:2048 + (q4 + 1) * 512],
                        ).then_inc(s_xh2, 16)
                    phase_inc(s)

                @blk.gpsimd
                def _(g):
                    phase_wait(g)
                    for idx, (w, half) in enumerate(jobs):
                        if idx >= 3:
                            g.wait_ge(s_jt2, idx - 2)
                            g.wait_ge(s_w2[idx % 3], 16 * (idx // 3))
                        g.dma_start(
                            wcol[idx % 3][:], wqT_r[:, :, w * 128:(w + 1) * 128]
                        ).then_inc(s_w2[idx % 3], 16)
                    phase_inc(g)

                @blk.tensor
                def _(t):
                    phase_wait(t)
                    for idx, (w, half) in enumerate(jobs):
                        t.wait_ge(s_w2[idx % 3], 16 * (idx // 3 + 1))
                        if idx < JW:
                            t.wait_ge(s_xh, 64)
                        else:
                            t.wait_ge(s_xh2, 64)
                        if idx >= 2:
                            t.wait_ge(s_ev2, idx - 1)
                            nred = sum(1 for j in bd_jobs if j <= idx - 2)
                            if nred:
                                t.wait_ge(s_bdr, nred)
                        last = None
                        for dt in range(DT):
                            for nb in range(4):
                                last = t.matmul(
                                    ps_q[idx % 2][:, nb * 512:(nb + 1) * 512],
                                    wcol[idx % 3][:, dt, :],
                                    xnh[:, dt, nb * 512:(nb + 1) * 512],
                                    start=(dt == 0), stop=(dt == DT - 1),
                                )
                        last.then_inc(s_jt2, 1)
                        pj = idx - 1
                        if pj in bd_rank:
                            r = bd_rank[pj]
                            t.wait_ge(s_ev2, pj + 1)
                            t.wait_ge(s_sqst, r + 1)
                            for nb in range(4):
                                t.matmul(ps_q[pj % 2][0:1, 0:512], onec_sb[:],
                                         sqst[:, nb * 512:(nb + 1) * 512],
                                         start=(nb == 0),
                                         stop=(nb == 3)).then_inc(s_bnd4, 1)
                    pj = len(jobs) - 1
                    if pj in bd_rank:
                        r = bd_rank[pj]
                        t.wait_ge(s_ev2, pj + 1)
                        t.wait_ge(s_sqst, r + 1)
                        for nb in range(4):
                            t.matmul(ps_q[pj % 2][0:1, 0:512], onec_sb[:],
                                     sqst[:, nb * 512:(nb + 1) * 512],
                                     start=(nb == 0), stop=(nb == 3)).then_inc(s_bnd4, 1)
                    t.wait_ge(s_ev2, len(jobs))
                    t.wait_ge(s_bdr, len(bd_jobs))
                    t.wait_ge(s_nbB, 3)
                    t.matmul(ps_q[0][:, 0:1], onerow_sb[:], bbr[:],
                             start=True, stop=True).then_inc(s_nbB, 1)
                    phase_inc(t)

                @blk.scalar
                def _(sc):
                    phase_wait(sc)
                    for idx, (w, half) in enumerate(jobs):
                        sc.wait_ge(s_jt2, idx + 1)
                        if idx >= 2:
                            sc.wait_ge(s_st2[idx % 2], 16 * (idx // 2))
                            nsq = sum(1 for j in bd_jobs if j <= idx - 2)
                            if nsq:
                                sc.wait_ge(s_sqst, nsq)
                        last = None
                        for nb in range(4):
                            last = sc.activation(
                                stage[idx % 2][:, nb * 512:(nb + 1) * 512],
                                ps_q[idx % 2][:, nb * 512:(nb + 1) * 512],
                                AF.Identity, bias=bqkv_sb[:, w:w + 1], scale=1.0,
                            )
                        last.then_inc(s_ev2, 1)
                        sc.wait_ge(s_ev2, idx + 1)
                        sc.dma_start(p2_dest(w, half),
                                     stage[idx % 2][:]).then_inc(s_st2[idx % 2], 16)
                    sc.wait_ge(s_st2[0], 16 * ((len(jobs) + 1) // 2))
                    sc.wait_ge(s_st2[1], 16 * (len(jobs) // 2))
                    sc.wait_ge(s_nbB, 1)
                    m1 = sedge(sc.activation(bbr[:], bbr[:], AF.Ln))
                    sc.wait_ge(sSelf, m1)
                    sc.activation(bbr[:], bbr[:], AF.Exp, scale=0.5,
                                  bias=cst_sb[0:1, 1:2]).then_inc(s_nbB, 1)
                    sc.wait_ge(s_nbB, 4)
                    sc.copy(negB_sb[:], ps_q[0][:, 0:1])
                    phase_inc(sc)

                @blk.vector
                def _(v):
                    phase_wait(v)
                    for r, j in enumerate(bd_jobs):
                        w, half = jobs[j]
                        v.wait_ge(s_ev2, j + 1)
                        if r >= 1:
                            v.wait_ge(s_bnd4, 4 * r)
                        v.tensor_mul(sqst[:], stage[j % 2][:],
                                     stage[j % 2][:]).then_inc(s_sqst, 1)
                        v.wait_ge(s_bnd4, 4 * (r + 1))
                        n1 = vedge(v.reduce_max(blkr[:], ps_q[j % 2][0:1, 0:512], axis=AX))
                        v.wait_ge(vSelf, n1)
                        if r == 0:
                            v.tensor_copy(qmax_r[:], blkr[:]).then_inc(s_bdr, 1)
                        elif r == first_k_rank:
                            v.tensor_copy(kmax_r[:], blkr[:]).then_inc(s_bdr, 1)
                        elif w < 32:
                            v.tensor_max(qmax_r[:], qmax_r[:], blkr[:]).then_inc(s_bdr, 1)
                        else:
                            v.tensor_max(kmax_r[:], kmax_r[:], blkr[:]).then_inc(s_bdr, 1)
                    v.wait_ge(s_bdr, len(bd_jobs))
                    v.tensor_mul(bbr[:], qmax_r[:], kmax_r[:]).then_inc(s_nbB, 1)
                    v.wait_ge(s_nbB, 2)
                    v.tensor_scalar_mul(bbr[:], bbr[:], -1.0).then_inc(s_nbB, 1)
                    phase_inc(v)
        PH["k"] += 1

        s_xn5 = [E(nc.semaphore(f"s_xn5{i}")) for i in range(2)]
        s_gw5 = E(nc.semaphore("s_gw5"))
        s_lg = E(nc.semaphore("s_lg"))
        s_vm = E(nc.semaphore("s_vm"))
        s_e8 = E(nc.semaphore("s_e8"))
        s_rt = E(nc.semaphore("s_rt"))
        s_tr = E(nc.semaphore("s_tr"))
        s_rc = E(nc.semaphore("s_rc"))
        s_sl5 = [E(nc.semaphore(f"s_sl5{i}")) for i in range(2)]
        s_im = E(nc.semaphore("s_im"))
        s_is5 = E(nc.semaphore("s_is5"))
        s_xw5 = [E(nc.semaphore(f"s_xw5{i}")) for i in range(2)]
        s_vl5 = [E(nc.semaphore(f"s_vl5{i}")) for i in range(2)]
        s_vt5 = E(nc.semaphore("s_vt5"))
        s_vc5 = E(nc.semaphore("s_vc5"))
        s_vw5 = [E(nc.semaphore(f"s_vw5{i}")) for i in range(2)]
        with ExitStack() as ph:
            PE = ph.enter_context
            vband = [PE(nc.sbuf_tensor(f"vband{i}", [128, S], BF16)) for i in range(2)]
            vstage = [PE(nc.sbuf_tensor(f"vstage{i}", [128, DT, 128], BF16)) for i in range(2)]
            ps_vt = PE(nc.psum_tensor("ps_vt", [128, 128], BF16))
            gw_sb = PE(nc.sbuf_tensor("gw_sb", [128, DT, 8], BF16))
            ics_sb = PE(nc.sbuf_tensor("ics_sb", [8, D], BF16))
            xn_t = [PE(nc.sbuf_tensor(f"xn_t{i}", [128, DT, 128], BF16)) for i in range(2)]
            e8 = PE(nc.sbuf_tensor("e8", [128, 8], F32))
            negmx = PE(nc.sbuf_tensor("negmx", [128, 1], F32))
            sm8 = PE(nc.sbuf_tensor("sm8", [128, 1], F32))
            rs8 = PE(nc.sbuf_tensor("rs8", [128, 1], F32))
            rout_bf = PE(nc.sbuf_tensor("rout_bf", [128, 8], BF16))
            xn_sl = [PE(nc.sbuf_tensor(f"xn_sl{i}", [128, DT, 512], BF16)) for i in range(2)]
            xs_sl = [PE(nc.sbuf_tensor(f"xs_sl{i}", [128, DT, 512], BF16)) for i in range(2)]
            ps_l = PE(nc.psum_tensor("ps_l", [128, 8], F32))
            ps_rt = PE(nc.psum_tensor("ps_rt", [8, 128], BF16))
            ps_i = [PE(nc.psum_tensor(f"ps_i{i}", [128, 512], F32)) for i in range(2)]
            with nc.Block() as blk:
                @blk.sync
                def _(s):
                    phase_wait(s)
                    for nt in range(NCH):
                        if nt >= 2:
                            s.wait_ge(s_lg, nt - 1)
                            s.wait_ge(s_xn5[nt % 2], 16 * (nt // 2))
                        s.dma_start(
                            xn_t[nt % 2][:], xnT_r[:, :, nt * 128:(nt + 1) * 128]
                        ).then_inc(s_xn5[nt % 2], 16)
                    for nb in range(4):
                        if nb >= 2:
                            s.wait_ge(s_is5, DT * (nb - 1))
                            s.wait_ge(s_sl5[nb % 2], 16 * (nb // 2))
                        s.dma_start(
                            xn_sl[nb % 2][:], xnT_r[:, :, nb * 512:(nb + 1) * 512]
                        ).then_inc(s_sl5[nb % 2], 16)
                    for g in range(NKV):
                        if g >= 2:
                            s.wait_ge(s_vt5, DT * (g - 1))
                            s.wait_ge(s_vl5[g % 2], 16 * (g // 2))
                        s.dma_start(
                            vband[g % 2][:], v_scr[g * 128:(g + 1) * 128, :]
                        ).then_inc(s_vl5[g % 2], 16)
                    phase_inc(s)

                @blk.gpsimd
                def _(g):
                    phase_wait(g)
                    g.dma_start(gw_sb[:], gwT_r[:]).then_inc(s_gw5, 16)
                    g.dma_start(ics_sb[:], ics8).then_inc(s_gw5, 16)
                    for nb in range(4):
                        g.wait_ge(s_is5, DT * (nb + 1))
                        if nb >= 2:
                            g.wait_ge(s_xw5[nb % 2], 16 * (nb // 2))
                        g.dma_start(
                            xsT_r[:, :, nb * 512:(nb + 1) * 512], xs_sl[nb % 2][:]
                        ).then_inc(s_xw5[nb % 2], 16)
                    g.wait_ge(s_xw5[0], 32)
                    g.wait_ge(s_xw5[1], 32)
                    for gg in range(NKV):
                        g.wait_ge(s_vc5, DT * (gg + 1))
                        if gg >= 2:
                            g.wait_ge(s_vw5[gg % 2], 16 * (gg // 2))
                        g.dma_start(
                            v_tok_scr[gg * S:(gg + 1) * S, :].rearrange(
                                "(kt p) d -> p kt d", p=128),
                            vstage[gg % 2][:],
                        ).then_inc(s_vw5[gg % 2], 16)
                    g.wait_ge(s_vw5[0], 16 * (NKV // 2))
                    g.wait_ge(s_vw5[1], 16 * (NKV // 2))
                    phase_inc(g)

                @blk.tensor
                def _(t):
                    phase_wait(t)
                    t.wait_ge(s_gw5, 32)
                    for nt in range(NCH):
                        t.wait_ge(s_xn5[nt % 2], 16 * (nt // 2 + 1))
                        if nt >= 1:
                            t.wait_ge(s_e8, nt)
                        last = None
                        for dt in range(DT):
                            last = t.matmul(ps_l[:], xn_t[nt % 2][:, dt, :], gw_sb[:, dt, :],
                                            start=(dt == 0), stop=(dt == DT - 1))
                        last.then_inc(s_lg, 1)
                        t.wait_ge(s_rt, nt + 1)
                        if nt >= 1:
                            t.wait_ge(s_rc, nt)
                        t.transpose(ps_rt[:], rout_bf[:], iden_sb[:]).then_inc(s_tr, 1)
                    t.wait_ge(s_rc, NCH)
                    for nb in range(4):
                        t.wait_ge(s_sl5[nb % 2], 16 * (nb // 2 + 1))
                        for dt in range(DT):
                            k = nb * DT + dt
                            if k >= 2:
                                t.wait_ge(s_is5, k - 1)
                            t.matmul(ps_i[k % 2][:], ics_sb[:, dt * 128:(dt + 1) * 128],
                                     routT_sb[:, nb * 512:(nb + 1) * 512],
                                     start=True, stop=True).then_inc(s_im, 1)
                    for g in range(NKV):
                        t.wait_ge(s_vl5[g % 2], 16 * (g // 2 + 1))
                        for kt in range(DT):
                            gi = g * DT + kt
                            if gi >= 1:
                                t.wait_ge(s_vc5, gi)
                            t.transpose(ps_vt[:], vband[g % 2][:, kt * 128:(kt + 1) * 128],
                                        iden_sb[:]).then_inc(s_vt5, 1)
                    phase_inc(t)

                @blk.vector
                def _(v):
                    phase_wait(v)
                    for nt in range(NCH):
                        v.wait_ge(s_lg, nt + 1)
                        if nt >= 1:
                            v.wait_ge(s_e8, nt)
                        v.reduce_max(negmx[:], ps_l[:], axis=AX, negate=True).then_inc(s_vm, 1)
                        v.wait_ge(s_e8, nt + 1)
                        if nt >= 1:
                            v.wait_ge(s_tr, nt)
                        n1 = vedge(v.reduce_sum(sm8[:], e8[:], axis=AX))
                        v.wait_ge(vSelf, n1)
                        n2 = vedge(v.reciprocal(rs8[:], sm8[:]))
                        v.wait_ge(vSelf, n2)
                        v.tensor_scalar_mul(rout_bf[:], e8[:], rs8[:]).then_inc(s_rt, 1)
                    for nb in range(4):
                        for dt in range(DT):
                            k = nb * DT + dt
                            v.wait_ge(s_im, k + 1)
                            if nb >= 2 and dt == 0:
                                v.wait_ge(s_xw5[nb % 2], 16 * (nb // 2))
                            v.tensor_mul(xs_sl[nb % 2][:, dt, :], xn_sl[nb % 2][:, dt, :],
                                         ps_i[k % 2][:]).then_inc(s_is5, 1)
                    phase_inc(v)

                @blk.scalar
                def _(sc):
                    phase_wait(sc)
                    for nt in range(NCH):
                        sc.wait_ge(s_vm, nt + 1)
                        if nt >= 1:
                            sc.wait_ge(s_rt, nt)
                        sc.activation(e8[:], ps_l[:], AF.Exp, bias=negmx[:],
                                      scale=1.0).then_inc(s_e8, 1)
                        sc.wait_ge(s_tr, nt + 1)
                        sc.copy(routT_sb[:, nt * 128:(nt + 1) * 128], ps_rt[:]).then_inc(s_rc, 1)
                    for g in range(NKV):
                        for kt in range(DT):
                            gi = g * DT + kt
                            sc.wait_ge(s_vt5, gi + 1)
                            if g >= 2 and kt == 0:
                                sc.wait_ge(s_vw5[g % 2], 16 * (g // 2))
                            sc.copy(vstage[g % 2][:, kt, :], ps_vt[:]).then_inc(s_vc5, 1)
                    phase_inc(sc)
        PH["k"] += 1

        s_c3 = E(nc.semaphore("s_c3"))
        s_graw = E(nc.semaphore("s_graw"))
        s_qkraw = E(nc.semaphore("s_qkraw"))
        s_vtk = [E(nc.semaphore(f"s_vtk{i}")) for i in range(2)]
        s_S = E(nc.semaphore("s_S"))
        s_P = E(nc.semaphore("s_P"))
        s_qbD = E(nc.semaphore("s_qbD"))
        s_avc = E(nc.semaphore("s_avc"))
        s_r3 = E(nc.semaphore("s_r3"))
        s_rb = E(nc.semaphore("s_rb"))
        s_N = E(nc.semaphore("s_N"))
        s_og = E(nc.semaphore("s_og"))
        TRI_G = sum(2 * (i + 1) for i in range(NCH))
        with ExitStack() as ph:
            PE = ph.enter_context
            cost_sb = PE(nc.sbuf_tensor("cost_sb", [128, S], F32))
            sint_sb = PE(nc.sbuf_tensor("sint_sb", [128, S], F32))
            mtri_sb = PE(nc.sbuf_tensor("mtri_sb", [128, 512], BF16))
            moth_sb = PE(nc.sbuf_tensor("moth_sb", [128, 512], BF16))
            kraw = PE(nc.sbuf_tensor("kraw", [128, S], BF16))
            kswap = PE(nc.sbuf_tensor("kswap", [128, S], BF16))
            qraw = PE(nc.sbuf_tensor("qraw", [128, 4, NOWN], BF16))
            qswap = PE(nc.sbuf_tensor("qswap", [128, 4, NOWN], BF16))
            krope = [PE(nc.sbuf_tensor(f"krope{i}", [128, S], BF16)) for i in range(2)]
            qrope = [PE(nc.sbuf_tensor(f"qrope{i}", [128, NCH, 4, 128], BF16)) for i in range(2)]
            vtok = [PE(nc.sbuf_tensor(f"vtok{i}", [128, DT, 128], BF16)) for i in range(2)]
            outg = [PE(nc.sbuf_tensor(f"outg{i}", [128, NCH, 4, 128], BF16)) for i in range(2)]
            pt = PE(nc.sbuf_tensor("pt", [128, 6, 512], BF16))
            t1b = [PE(nc.sbuf_tensor(f"t1b{i}", [128, 512], BF16)) for i in range(2)]
            av_sb = [PE(nc.sbuf_tensor(f"av_sb{i}", [128, 512], F32)) for i in range(2)]
            dr_bf = [PE(nc.sbuf_tensor(f"dr_bf{i}", [1, 512], BF16)) for i in range(2)]
            dr_ln = PE(nc.sbuf_tensor("dr_ln", [1, 512], F32))
            ps_s = PE(nc.psum_tensor("ps_s", [128, 2048], F32))
            ps_av = [PE(nc.psum_tensor(f"ps_av{i}", [128, 512], F32)) for i in range(2)]
            ps_d = [PE(nc.psum_tensor(f"ps_d{i}", [1, 512], F32)) for i in range(2)]

            def kt_list(i):
                return [("own", j) for j in range(i + 1)] + [("oth", j) for j in range(i + 1)]

            def rope_srcs(k):
                if k < 8:
                    cols = slice(k * 512, (k + 1) * 512)
                    return kraw[:, cols], kswap[:, cols], cols
                h, bq = divmod(k - 8, 4)
                cols = slice(bq * 512, (bq + 1) * 512)
                return qraw[:, h, cols], qswap[:, h, cols], cols

            def rope_dst(g_, k):
                if k < 8:
                    return krope[g_ % 2][:, k * 512:(k + 1) * 512]
                h, bq = divmod(k - 8, 4)
                return qrope[g_ % 2][:, 4 * bq:4 * bq + 4, h, :]

            with nc.Block() as blk:
                @blk.sync
                def _(s):
                    phase_wait(s)
                    s.dma_start(cost_sb[:], cost).then_inc(s_c3, 16)
                    s.dma_start(sint_sb[:], sint).then_inc(s_c3, 16)
                    s.dma_start(mtri_sb[:], mtri).then_inc(s_c3, 16)
                    s.dma_start(moth_sb[:], moth).then_inc(s_c3, 16)
                    for g in range(NKV):
                        if g >= 2:
                            s.wait_ge(s_N, (g - 1) * NCH)
                            s.wait_ge(s_vtk[g % 2], 16 * (g // 2))
                        s.dma_start(vtok[g % 2][:], vtok_r[g]).then_inc(s_vtk[g % 2], 16)
                    phase_inc(s)

                @blk.gpsimd
                def _(g_):
                    phase_wait(g_)
                    for g in range(NKV):
                        if g >= 1:
                            g_.wait_ge(s_qkraw, g)
                            g_.wait_ge(s_graw, 96 * g)
                        g_.dma_start(kraw[:], k_scr[g * 128:(g + 1) * 128, :]).then_inc(s_graw, 16)
                        g_.dma_start(qraw[:], q_scr_r[:, 4 * g:4 * g + 4, :]).then_inc(s_graw, 16)
                        g_.dma_start(kswap[0:64, :],
                                     k_scr[g * 128 + 64:(g + 1) * 128, :]).then_inc(s_graw, 16)
                        g_.dma_start(kswap[64:128, :],
                                     k_scr[g * 128:g * 128 + 64, :]).then_inc(s_graw, 16)
                        g_.dma_start(qswap[0:64, :, :],
                                     q_scr_r[64:128, 4 * g:4 * g + 4, :]).then_inc(s_graw, 16)
                        g_.dma_start(qswap[64:128, :, :],
                                     q_scr_r[0:64, 4 * g:4 * g + 4, :]).then_inc(s_graw, 16)
                        if g >= 1:
                            g_.wait_ge(s_N, g * NCH)
                            g_.wait_ge(s_og, 64 * (g - 1))
                            for h in range(4):
                                g_.dma_start(
                                    outT_r[:, 4 * (g - 1) + h, :].rearrange(
                                        "p (a b) -> p a b", b=128),
                                    outg[(g - 1) % 2][:, :, h, :],
                                ).then_inc(s_og, 16)
                    g_.wait_ge(s_N, NKV * NCH)
                    g_.wait_ge(s_og, 64 * (NKV - 1))
                    for h in range(4):
                        g_.dma_start(
                            outT_r[:, 4 * (NKV - 1) + h, :].rearrange(
                                "p (a b) -> p a b", b=128),
                            outg[(NKV - 1) % 2][:, :, h, :],
                        ).then_inc(s_og, 16)
                    g_.wait_ge(s_og, 16 * 4 * NKV)
                    phase_inc(g_)

                @blk.tensor
                def _(t):
                    phase_wait(t)
                    t.wait_ge(s_c3, 64)

                    deferred = []
                    avd_pend = []
                    tglob = 0
                    for g in range(NKV):
                        t.wait_ge(s_qkraw, g + 1)
                        t.wait_ge(s_vtk[g % 2], 16 * (g // 2 + 1))
                        for i in range(NCH):
                            Q = g * NCH + i
                            kts = kt_list(i)
                            ntr = len(kts)

                            def mk_avd(pidx, base=tglob, Q=Q, g=g, kts=kts, ntr=ntr):
                                def op():
                                    side, ch = kts[pidx]
                                    vslot = ch if side == "own" else 16 + ch
                                    pfirst = pidx == 0
                                    plast = pidx == ntr - 1
                                    tc = base + pidx
                                    t.wait_ge(s_P, tc // 2 + 1)
                                    if pfirst and Q >= 2:
                                        t.wait_ge(s_N, Q - 1)
                                    t.matmul(ps_av[Q % 2][:], vtok[g % 2][:, vslot, :],
                                             pt[:, tc % 6, :], start=pfirst, stop=plast)
                                    if pidx % 2 == 1 or plast:
                                        for pp in ([pidx - 1, pidx] if pidx % 2 == 1
                                                   else [pidx]):
                                            tc2 = base + pp
                                            mmd = t.matmul(ps_d[Q % 2][:], onec_sb[:],
                                                           pt[:, tc2 % 6, :],
                                                           start=(pp == 0),
                                                           stop=(pp == ntr - 1))
                                            if pp == ntr - 1:
                                                mmd.then_inc(s_qbD, 1)
                                return op

                            for j, (side, ch) in enumerate(kts):
                                tcur = tglob + j
                                masked = (j == i) or (j == ntr - 1)
                                if tcur >= 4:
                                    t.wait_ge(s_P, tcur // 2 - 1)
                                kcols = ch * 128 if side == "own" else 2048 + ch * 128
                                mm = t.matmul(
                                    ps_s[:, (tcur % 4) * 512:(tcur % 4 + 1) * 512],
                                    krope[g % 2][:, kcols:kcols + 128],
                                    qrope[g % 2][:, i, :, :].rearrange("p a b -> p (a b)"),
                                    start=True, stop=(not masked),
                                )
                                if masked:
                                    mk = mtri_sb if j == i else moth_sb
                                    mm = t.matmul(ps_s[:, (tcur % 4) * 512:(tcur % 4 + 1) * 512],
                                                  iden_sb[:], mk[:], start=False, stop=True)
                                mm.then_inc(s_S, 1)
                                if j == 5 and deferred:
                                    deferred.pop(0)()
                                avd_pend.append(mk_avd(j))
                                if len(avd_pend) > 4:
                                    avd_pend.pop(0)()
                            if i <= 2:
                                while avd_pend:
                                    avd_pend.pop(0)()
                            tglob += ntr

                            def rb_op(Q=Q):
                                t.wait_ge(s_avc, Q + 1)
                                t.wait_ge(s_r3, Q + 1)
                                t.matmul(ps_av[Q % 2][:], onerow_bf_sb[:], dr_bf[Q % 2][:],
                                         start=True, stop=True).then_inc(s_rb, 1)
                            deferred.append(rb_op)
                            if i <= 2:
                                deferred.pop(0)()
                        while avd_pend:
                            avd_pend.pop(0)()
                        while deferred:
                            deferred.pop(0)()
                    phase_inc(t)

                @blk.scalar
                def _(sc):
                    phase_wait(sc)
                    tglob = 0
                    pend_av = None
                    for g in range(NKV):
                        for i in range(NCH):
                            Q = g * NCH + i
                            ntr = 2 * (i + 1)
                            for j in range(0, ntr, 2):
                                tcur = tglob + j
                                sc.wait_ge(s_S, tcur + 2)
                                half = (tcur % 4) // 2
                                slot = tcur % 6
                                sc.activation(
                                    pt[:, slot:slot + 2, :].rearrange("p a b -> p (a b)"),
                                    ps_s[:, half * 1024:(half + 1) * 1024],
                                    AF.Exp, bias=negB_sb[:], scale=ISQ,
                                ).then_inc(s_P, 1)
                                if j == 0 and pend_av is not None:
                                    pend_av()
                                    pend_av = None
                            tglob += ntr

                            def av_op(Q=Q):
                                sc.wait_ge(s_qbD, Q + 1)
                                sc.copy(av_sb[Q % 2][:], ps_av[Q % 2][:]).then_inc(s_avc, 1)
                                m1 = sedge(sc.activation(dr_ln[:], ps_d[Q % 2][:], AF.Ln,
                                                         scale=2.0 ** 64))
                                sc.wait_ge(sSelf, m1)
                                sc.activation(dr_bf[Q % 2][:], dr_ln[:], AF.Exp,
                                              scale=-1.0,
                                              bias=cst_sb[0:1, 2:3]).then_inc(s_r3, 1)
                            if i <= 2:
                                av_op()
                            else:
                                pend_av = av_op
                        if pend_av is not None:
                            pend_av()
                            pend_av = None
                    phase_inc(sc)

                @blk.vector
                def _(v):
                    phase_wait(v)
                    v.wait_ge(s_c3, 64)

                    def mk_rope(g_, k):
                        def op():
                            raw, swp, cols = rope_srcs(k)
                            dst = rope_dst(g_, k)
                            v.wait_ge(s_graw, 96 * (g_ + 1))
                            if g_ >= 2:
                                v.wait_ge(s_S, TRI_G * (g_ - 1))
                            n1 = vedge(v.tensor_mul(t1b[k % 2][:], swp, sint_sb[:, cols]))
                            if len(dst.shape) == 3:
                                n2 = vedge(v.tensor_mul(
                                    dst, raw.rearrange("p (a b) -> p a b", b=128),
                                    cost_sb[:, cols].rearrange("p (a b) -> p a b", b=128)))
                                v.wait_ge(vSelf, n2)
                                v.wait_ge(vSelf, n1)
                                mm = v.tensor_add(
                                    dst, dst,
                                    t1b[k % 2][:].rearrange("p (a b) -> p a b", b=128))
                            else:
                                n2 = vedge(v.tensor_mul(dst, raw, cost_sb[:, cols]))
                                v.wait_ge(vSelf, n2)
                                v.wait_ge(vSelf, n1)
                                mm = v.tensor_add(dst, dst, t1b[k % 2][:])
                            if k == 23:
                                mm.then_inc(s_qkraw, 1)
                        return op

                    def v_buckets(g_):
                        bks = [[] for _ in range(NCH)]
                        for k in range(24):
                            bks[min(k // 2 + 1, NCH - 1)].append(mk_rope(g_, k))
                        return bks

                    for op in [o for bk in v_buckets(0) for o in bk]:
                        op()

                    for g in range(NKV):
                        buckets = v_buckets(g + 1) if g + 1 < NKV else [[] for _ in range(NCH)]
                        for i in range(NCH):
                            Q = g * NCH + i
                            if i == 0 and g >= 2:
                                v.wait_ge(s_og, 64 * (g - 1))
                            v.wait_ge(s_rb, Q + 1)
                            v.tensor_mul(
                                outg[g % 2][:, i, :, :].rearrange("p a b -> p (a b)"),
                                av_sb[Q % 2][:], ps_av[Q % 2][:],
                            ).then_inc(s_N, 1)
                            for op in buckets[i]:
                                op()
                    phase_inc(v)
        PH["k"] += 1

        s_or = E(nc.semaphore("s_or"))
        s_w4 = [E(nc.semaphore(f"s_w4{i}")) for i in range(3)]
        s_xt4 = [E(nc.semaphore(f"s_xt4{i}")) for i in range(2)]
        s_mt4 = E(nc.semaphore("s_mt4"))
        s_ev4 = E(nc.semaphore("s_ev4"))
        s_pw4 = [E(nc.semaphore(f"s_pw4{i}")) for i in range(2)]
        with ExitStack() as ph:
            PE = ph.enter_context
            out_res = PE(nc.sbuf_tensor("out_res", [128, DT, NOWN], BF16))
            wocol = [PE(nc.sbuf_tensor(f"wocol{i}", [128, DT, 128], BF16)) for i in range(3)]
            xt_t = [PE(nc.sbuf_tensor(f"xt_t{i}", [128, NOWN], F32)) for i in range(2)]
            stage_o = [PE(nc.sbuf_tensor(f"stage_o{i}", [128, NOWN], F32)) for i in range(2)]
            tmp_u = [PE(nc.sbuf_tensor(f"tmp_u{i}", [128, 512], F32)) for i in range(2)]
            ps_o = [PE(nc.psum_tensor(f"ps_o{i}", [128, 2048], F32)) for i in range(2)]
            with nc.Block() as blk:
                @blk.sync
                def _(s):
                    phase_wait(s)
                    for q4 in range(4):
                        s.dma_start(
                            out_res[:, :, q4 * 512:(q4 + 1) * 512],
                            outT_r[:, :, q4 * 512:(q4 + 1) * 512],
                        ).then_inc(s_or, 16)
                    for mt in range(DT):
                        if mt >= 2:
                            s.wait_ge(s_ev4, mt - 1)
                            s.wait_ge(s_xt4[mt % 2], 16 * (mt // 2))
                        s.dma_start(
                            xt_t[mt % 2][:], xT[mt * 128:(mt + 1) * 128, 0:NOWN]
                        ).then_inc(s_xt4[mt % 2], 16)
                    phase_inc(s)

                @blk.gpsimd
                def _(g):
                    phase_wait(g)
                    for mt in range(DT):
                        if mt >= 3:
                            g.wait_ge(s_mt4, mt - 2)
                            g.wait_ge(s_w4[mt % 3], 16 * (mt // 3))
                        g.dma_start(
                            wocol[mt % 3][:], woT_r[:, :, mt * 128:(mt + 1) * 128]
                        ).then_inc(s_w4[mt % 3], 16)
                    phase_inc(g)

                @blk.tensor
                def _(t):
                    phase_wait(t)
                    t.wait_ge(s_or, 64)
                    for mt in range(DT):
                        t.wait_ge(s_w4[mt % 3], 16 * (mt // 3 + 1))
                        if mt >= 2:
                            t.wait_ge(s_ev4, mt - 1)
                        last = None
                        for dt in range(DT):
                            for nb in range(4):
                                last = t.matmul(
                                    ps_o[mt % 2][:, nb * 512:(nb + 1) * 512],
                                    wocol[mt % 3][:, dt, :],
                                    out_res[:, dt, nb * 512:(nb + 1) * 512],
                                    start=(dt == 0), stop=(dt == DT - 1),
                                )
                        last.then_inc(s_mt4, 1)
                    phase_inc(t)

                @blk.vector
                def _(v):
                    phase_wait(v)
                    for mt in range(DT):
                        v.wait_ge(s_mt4, mt + 1)
                        v.wait_ge(s_xt4[mt % 2], 16 * (mt // 2 + 1))
                        if mt >= 2:
                            v.wait_ge(s_pw4[mt % 2], 16 * (mt // 2))
                        last = None
                        for nb in range(4):
                            cols = slice(nb * 512, (nb + 1) * 512)
                            n1 = vedge(v.tensor_scalar_add(tmp_u[nb % 2][:],
                                                           ps_o[mt % 2][:, cols],
                                                           bsum_sb[:, mt:mt + 1]))
                            v.wait_ge(vSelf, n1)
                            last = v.tensor_add(stage_o[mt % 2][:, cols], tmp_u[nb % 2][:],
                                                xt_t[mt % 2][:, cols])
                        last.then_inc(s_ev4, 1)
                    phase_inc(v)

                @blk.scalar
                def _(sc):
                    phase_wait(sc)
                    for mt in range(DT):
                        sc.wait_ge(s_ev4, mt + 1)
                        if mt >= 2:
                            sc.wait_ge(s_pw4[mt % 2], 16 * (mt // 2))
                        sc.dma_start(
                            projT_scr[mt * 128:(mt + 1) * 128, :], stage_o[mt % 2][:]
                        ).then_inc(s_pw4[mt % 2], 16)
                    sc.wait_ge(s_pw4[0], 16 * (DT // 2))
                    sc.wait_ge(s_pw4[1], 16 * (DT // 2))
                    phase_inc(sc)
        PH["k"] += 1

        s_xs6 = E(nc.semaphore("s_xs6"))
        s_w6 = E(nc.semaphore("s_w6"))
        s_sg6 = E(nc.semaphore("s_sg6"))
        s_os6 = E(nc.semaphore("s_os6"))
        s_oc6 = E(nc.semaphore("s_oc6"))
        s_jt6 = E(nc.semaphore("s_jt6"))
        s_fin6 = E(nc.semaphore("s_fin6"))
        s_pj6 = [E(nc.semaphore(f"s_pj6{i}")) for i in range(2)]
        s_out = [E(nc.semaphore(f"s_out{i}")) for i in range(2)]
        with ExitStack() as ph:
            PE = ph.enter_context
            xs_res = PE(nc.sbuf_tensor("xs_res", [128, DT, NOWN], BF16))
            wcol_f = PE(nc.sbuf_tensor("wcol_f", [128, 16, 128], F32))
            wsign = [PE(nc.sbuf_tensor(f"wsign{i}", [128, DT, 128], BF16)) for i in range(2)]
            os_sb = [PE(nc.sbuf_tensor(f"os_sb{i}", [128, 2048], BF16)) for i in range(2)]
            proj_t = [PE(nc.sbuf_tensor(f"proj_t{i}", [128, NOWN], F32)) for i in range(2)]
            out_st = [PE(nc.sbuf_tensor(f"out_st{i}", [128, NOWN], F32)) for i in range(2)]
            ps_f = [PE(nc.psum_tensor(f"ps_f{i}", [128, 2048], F32)) for i in range(2)]
            with nc.Block() as blk:
                @blk.sync
                def _(s):
                    phase_wait(s)
                    for q4 in range(4):
                        s.dma_start(
                            xs_res[:, :, q4 * 512:(q4 + 1) * 512],
                            xsT_r[:, :, q4 * 512:(q4 + 1) * 512],
                        ).then_inc(s_xs6, 16)
                    for jt in range(DT):
                        if jt >= 2:
                            s.wait_ge(s_fin6, 4 * (jt - 1))
                            s.wait_ge(s_pj6[jt % 2], 16 * (jt // 2))
                        s.dma_start(
                            proj_t[jt % 2][:], projT_scr[jt * 128:(jt + 1) * 128, :]
                        ).then_inc(s_pj6[jt % 2], 16)
                    s.wait_ge(s_out[0], 16 * (DT // 2))
                    s.wait_ge(s_out[1], 16 * (DT // 2))
                    phase_inc(s)

                @blk.gpsimd
                def _(g):
                    phase_wait(g)
                    for jt in range(DT):
                        for hf in range(2):
                            k = 2 * jt + hf
                            if k >= 1:
                                g.wait_ge(s_sg6, k)
                                g.wait_ge(s_w6, 16 * k)
                            g.dma_start(
                                wcol_f[:],
                                wfT_r[:, 16 * hf:16 * (hf + 1), jt * 128:(jt + 1) * 128],
                            ).then_inc(s_w6, 16)
                    phase_inc(g)

                @blk.scalar
                def _(sc):
                    phase_wait(sc)
                    for jt in range(DT):
                        sc.wait_ge(s_os6, jt + 1)
                        last = None
                        for nb in range(4):
                            last = sc.copy(os_sb[jt % 2][:, nb * 512:(nb + 1) * 512],
                                           ps_f[jt % 2][:, nb * 512:(nb + 1) * 512])
                        last.then_inc(s_oc6, 1)
                        for hf in range(2):
                            sc.wait_ge(s_w6, 16 * (2 * jt + hf + 1))
                            if jt >= 2 and hf == 0:
                                sc.wait_ge(s_jt6, jt - 1)
                            sc.activation(wsign[jt % 2][:, 16 * hf:16 * (hf + 1), :],
                                          wcol_f[:], AF.Sign).then_inc(s_sg6, 1)
                        if jt >= 1:
                            sc.wait_ge(s_fin6, 4 * jt)
                            if jt >= 3:
                                sc.wait_ge(s_out[(jt - 1) % 2], 16 * ((jt - 1) // 2))
                            sc.dma_start(
                                yT[(jt - 1) * 128:jt * 128, :], out_st[(jt - 1) % 2][:]
                            ).then_inc(s_out[(jt - 1) % 2], 16)
                    sc.wait_ge(s_fin6, 4 * DT)
                    sc.wait_ge(s_out[(DT - 1) % 2], 16 * ((DT - 1) // 2))
                    sc.dma_start(
                        yT[(DT - 1) * 128:DT * 128, :], out_st[(DT - 1) % 2][:]
                    ).then_inc(s_out[(DT - 1) % 2], 16)
                    phase_inc(sc)

                @blk.tensor
                def _(t):
                    phase_wait(t)
                    t.wait_ge(s_xs6, 64)

                    def os_block(jj):
                        t.wait_ge(s_fin6, 4 * (jj - 1)) if jj >= 1 else None
                        last = None
                        for nb in range(4):
                            last = t.matmul(ps_f[jj % 2][:, nb * 512:(nb + 1) * 512],
                                            ocs_sb[0:8, jj * 128:(jj + 1) * 128],
                                            routT_sb[:, nb * 512:(nb + 1) * 512],
                                            start=True, stop=True)
                        last.then_inc(s_os6, 1)

                    os_block(0)
                    for jt in range(DT):
                        t.wait_ge(s_oc6, jt + 1)
                        t.wait_ge(s_sg6, 2 * (jt + 1))
                        last = None
                        for dt in range(DT):
                            if dt == 16 and jt + 1 < DT:
                                os_block(jt + 1)
                            for nb in range(4):
                                last = t.matmul(
                                    ps_f[jt % 2][:, nb * 512:(nb + 1) * 512],
                                    wsign[jt % 2][:, dt, :],
                                    xs_res[:, dt, nb * 512:(nb + 1) * 512],
                                    start=(dt == 0), stop=(dt == DT - 1),
                                )
                        last.then_inc(s_jt6, 1)
                    phase_inc(t)

                @blk.vector
                def _(v):
                    phase_wait(v)
                    for jt in range(DT):
                        v.wait_ge(s_jt6, jt + 1)
                        v.wait_ge(s_pj6[jt % 2], 16 * (jt // 2 + 1))
                        if jt >= 2:
                            v.wait_ge(s_out[jt % 2], 16 * (jt // 2))
                        for nb in range(4):
                            cols = slice(nb * 512, (nb + 1) * 512)
                            n1 = vedge(v.tensor_mul(out_st[jt % 2][:, cols],
                                                    ps_f[jt % 2][:, cols],
                                                    os_sb[jt % 2][:, cols]))
                            v.wait_ge(vSelf, n1)
                            v.tensor_add(out_st[jt % 2][:, cols], out_st[jt % 2][:, cols],
                                         proj_t[jt % 2][:, cols]).then_inc(s_fin6, 1)
                    phase_inc(v)
        PH["k"] += 1

    return nc


def _host_prep(
    L0_attn_x, L0_attn_w_qkv, L0_attn_b_qkv, L0_attn_w_o, L0_attn_b_o,
    L0_ffn_weight, L0_ffn_bias, L0_ffn_gate_w, L0_ffn_in_channel_scale,
    L0_ffn_out_channel_scale,
):
    x = np.asarray(L0_attn_x, np.float32)
    wq = np.asarray(L0_attn_w_qkv, np.float32)
    bq = np.asarray(L0_attn_b_qkv, np.float32)
    wo = np.asarray(L0_attn_w_o, np.float32)
    bo_ = np.asarray(L0_attn_b_o, np.float32)
    wf = np.asarray(L0_ffn_weight, np.float32)
    bf_ = np.asarray(L0_ffn_bias, np.float32)
    gw = np.asarray(L0_ffn_gate_w, np.float32)
    ics = np.asarray(L0_ffn_in_channel_scale, np.float32)
    ocs = np.asarray(L0_ffn_out_channel_scale, np.float32)

    sigma = np.concatenate([np.arange(0, HD, 2), np.arange(1, HD, 2)])
    qk_perm = np.arange(6144)
    for hh in range(40):
        qk_perm[hh * HD:(hh + 1) * HD] = hh * HD + sigma
    wqT = np.ascontiguousarray(wq.T[:, qk_perm])
    woT = np.ascontiguousarray(wo.T)
    wfT = np.ascontiguousarray(wf.T)
    gwT = np.ascontiguousarray(gw.T)
    bqkv_t = np.ascontiguousarray(bq[qk_perm].reshape(JW, 128).T)
    bo_t = np.ascontiguousarray(bo_.reshape(DT, 128).T)
    bf_t = np.ascontiguousarray(bf_.reshape(DT, 128).T)

    iden_m = np.eye(128, dtype=ml_dtypes.bfloat16)
    perm_m = np.zeros((128, 128), dtype=ml_dtypes.bfloat16)
    for i in range(64):
        perm_m[2 * i, 2 * i + 1] = 1
        perm_m[2 * i + 1, 2 * i] = 1
    onec_m = np.ones((128, 1), dtype=ml_dtypes.bfloat16)
    onerow_m = np.ones((1, 128), dtype=np.float32)
    onerow_bf_m = np.ones((1, 128), dtype=ml_dtypes.bfloat16)
    tri = np.where(np.arange(128)[:, None] <= np.arange(128)[None, :], 0.0, MBIG)
    mtri_m = np.tile(tri, (1, 4)).astype(ml_dtypes.bfloat16)

    idx = np.arange(S).reshape(32, 128)
    inv_freq = 1.0 / (10000.0 ** (np.arange(0, HD, 2, dtype=np.float64) / HD))

    in_maps = []
    for c in range(8):
        b, p = divmod(c, 2)
        order = np.concatenate([idx[p::2].ravel(), idx[1 - p::2].ravel()])
        xT_store = np.ascontiguousarray(x[b].T[:, order])
        ang = order[None, :].astype(np.float64) * inv_freq[:, None]
        cos_t = np.empty((128, S), np.float64)
        sin_t = np.empty((128, S), np.float64)
        cos_t[0:64] = np.cos(ang)
        cos_t[64:128] = np.cos(ang)
        sin_t[0:64] = -np.sin(ang)
        sin_t[64:128] = np.sin(ang)
        moth_m = np.full((128, 512), MBIG if p == 0 else 0.0, dtype=ml_dtypes.bfloat16)
        in_maps.append({
            "xT": xT_store, "wqT": wqT, "woT": woT, "wfT": wfT, "gwT": gwT,
            "ics8": ics, "ocs8": ocs, "bqkv": bqkv_t, "bo": bo_t, "bff": bf_t,
            "cost": cos_t.astype(np.float32), "sint": sin_t.astype(np.float32),
            "mtri": mtri_m, "moth": moth_m, "perm": perm_m, "iden": iden_m,
            "onec": onec_m, "onerow": onerow_m, "onerow_bf": onerow_bf_m,
        })
    return in_maps


def kernel(**inputs):
    in_maps = _host_prep(**inputs)
    nc = _build()
    res = run_bass_kernel_spmd(nc, in_maps, core_ids=list(range(8)))
    idx = np.arange(S).reshape(32, 128)
    out = np.empty((B, S, D), np.float32)
    for c in range(8):
        b, p = divmod(c, 2)
        own = idx[p::2].ravel()
        out[b][own, :] = res.results[c]["yT"].T
    return out


# version 09
# speedup vs baseline: 1.1141x; 1.0227x over previous
import math
import numpy as np
import ml_dtypes
from contextlib import ExitStack

import concourse.bass as bass
import concourse.mybir as mybir
from concourse.bass import broadcast_tensor_aps
from concourse.bass_utils import run_bass_kernel_spmd

F32 = mybir.dt.float32
BF16 = mybir.dt.bfloat16
AF = mybir.ActivationFunctionType
AX = mybir.AxisListType.X

D = 4096
S = 4096
B = 4
HD = 128
NKV = 8
NOWN = 2048
NCH = 16
DT = 32
JW = 48
ISQ = 1.0 / math.sqrt(HD)
MBIG = -30000.0
EPS = 1e-5


def _build(dbg=frozenset()):
    nc = bass.Bass("TRN2", target_bir_lowering=False)

    def dram(name, shape, dt, kind):
        if kind == "Internal" and name in dbg:
            kind = "ExternalOutput"
        return nc.dram_tensor(name, shape, dt, kind=kind).ap()

    xT = dram("xT", [D, S], F32, "ExternalInput")
    wqT = dram("wqT", [D, 6144], F32, "ExternalInput")
    woT = dram("woT", [D, D], F32, "ExternalInput")
    wfT = dram("wfT", [D, D], F32, "ExternalInput")
    gwT = dram("gwT", [D, 8], F32, "ExternalInput")
    ics8 = dram("ics8", [8, D], F32, "ExternalInput")
    ocs8 = dram("ocs8", [8, D], F32, "ExternalInput")
    bqkv = dram("bqkv", [128, JW], F32, "ExternalInput")
    bo = dram("bo", [128, DT], F32, "ExternalInput")
    bff = dram("bff", [128, DT], F32, "ExternalInput")
    cost = dram("cost", [128, S], F32, "ExternalInput")
    sint = dram("sint", [128, S], F32, "ExternalInput")
    mtri = dram("mtri", [128, 512], BF16, "ExternalInput")
    moth = dram("moth", [128, 512], BF16, "ExternalInput")
    perm = dram("perm", [128, 128], BF16, "ExternalInput")
    iden = dram("iden", [128, 128], BF16, "ExternalInput")
    onec = dram("onec", [128, 1], BF16, "ExternalInput")
    onerow = dram("onerow", [1, 128], F32, "ExternalInput")
    onerow_bf = dram("onerow_bf", [1, 128], BF16, "ExternalInput")
    yT = dram("yT", [D, NOWN], F32, "ExternalOutput")

    xnT_scr = dram("xnT_scr", [D, S], BF16, "Internal")
    q_scr = dram("q_scr", [D, NOWN], BF16, "Internal")
    k_scr = dram("k_scr", [NKV * HD, S], BF16, "Internal")
    v_scr = dram("v_scr", [NKV * HD, S], BF16, "Internal")
    outT_scr = dram("outT_scr", [D, NOWN], BF16, "Internal")
    projT_scr = dram("projT_scr", [D, NOWN], F32, "Internal")
    xsT_scr = dram("xsT_scr", [D, NOWN], BF16, "Internal")
    v_tok_scr = dram("v_tok_scr", [NKV * S, HD], BF16, "Internal")

    xnT_r = xnT_scr.rearrange("(dt part) s -> part dt s", part=128)
    xT_r = xT.rearrange("(dt part) s -> part dt s", part=128)
    wqT_r = wqT.rearrange("(dt part) j -> part dt j", part=128)
    woT_r = woT.rearrange("(dt part) j -> part dt j", part=128)
    wfT_r = wfT.rearrange("(dt part) j -> part dt j", part=128)
    gwT_r = gwT.rearrange("(dt part) j -> part dt j", part=128)
    q_scr_r = q_scr.rearrange("(h part) s -> part h s", part=128)
    outT_r = outT_scr.rearrange("(dt part) s -> part dt s", part=128)
    xsT_r = xsT_scr.rearrange("(dt part) s -> part dt s", part=128)
    vtok_r = v_tok_scr.rearrange("(g kt p) d -> g p kt d", p=128, kt=DT)

    with ExitStack() as top:
        E = top.enter_context
        bqkv_sb = E(nc.sbuf_tensor("bqkv_sb", [128, JW], F32))
        bo_sb = E(nc.sbuf_tensor("bo_sb", [128, DT], F32))
        bff_sb = E(nc.sbuf_tensor("bff_sb", [128, DT], F32))
        bsum_sb = E(nc.sbuf_tensor("bsum_sb", [128, DT], F32))
        iden_sb = E(nc.sbuf_tensor("iden_sb", [128, 128], BF16))
        onec_sb = E(nc.sbuf_tensor("onec_sb", [128, 1], BF16))
        onerow_sb = E(nc.sbuf_tensor("onerow_sb", [1, 128], F32))
        onerow_bf_sb = E(nc.sbuf_tensor("onerow_bf_sb", [1, 128], BF16))
        ocs_sb = E(nc.sbuf_tensor("ocs_sb", [8, D], BF16))
        routT_sb = E(nc.sbuf_tensor("routT_sb", [8, NOWN], BF16))
        negB_sb = E(nc.sbuf_tensor("negB_sb", [128, 1], F32))
        cst_sb = E(nc.sbuf_tensor("cst_sb", [1, 8], F32))

        sB = E(nc.semaphore("sB"))
        vSelf = E(nc.semaphore("vSelf"))
        sSelf = E(nc.semaphore("sSelf"))
        SELF = {"v": 0, "s": 0}

        def vedge(instr):
            SELF["v"] += 1
            instr.then_inc(vSelf, 1)
            return SELF["v"]

        def sedge(instr):
            SELF["s"] += 1
            instr.then_inc(sSelf, 1)
            return SELF["s"]

        PH = {"k": 0}

        def phase_wait(e):
            e.wait_ge(sB, 5 * PH["k"])

        def phase_inc(e):
            e.sem_inc(sB, 1)

        s_g = E(nc.semaphore("s_g"))
        s_g2 = E(nc.semaphore("s_g2"))
        with nc.Block() as blk:
            @blk.sync
            def _(s):
                for dst, src in [
                    (bqkv_sb, bqkv), (bo_sb, bo), (bff_sb, bff),
                    (iden_sb, iden), (onec_sb, onec), (onerow_sb, onerow),
                    (onerow_bf_sb, onerow_bf),
                ]:
                    s.dma_start(dst[:], src).then_inc(s_g, 16)
                phase_inc(s)

            @blk.gpsimd
            def _(g):
                g.dma_start(ocs_sb[:], ocs8).then_inc(s_g2, 16)
                phase_inc(g)

            @blk.vector
            def _(v):
                v.wait_ge(s_g, 112)
                v.wait_ge(s_g2, 16)
                v.tensor_add(bsum_sb[:], bo_sb[:], bff_sb[:])
                v.memset(cst_sb[0:1, 0:1], EPS)
                v.memset(cst_sb[0:1, 1:2], math.log(ISQ))
                v.memset(cst_sb[0:1, 2:3], 64.0 * math.log(2.0))
                phase_inc(v)

            @blk.tensor
            def _(t):
                phase_inc(t)

            @blk.scalar
            def _(sc):
                phase_inc(sc)
        PH["k"] += 1

        NG = 16
        GSZ = 256
        s_x = [E(nc.semaphore(f"s_x{i}")) for i in range(2)]
        s_v1 = E(nc.semaphore("s_v1"))
        s_sq1 = E(nc.semaphore("s_sq1"))
        s_t1 = E(nc.semaphore("s_t1"))
        s_v2 = E(nc.semaphore("s_v2"))
        s_v3 = E(nc.semaphore("s_v3"))
        s_sc0 = E(nc.semaphore("s_sc0"))
        s_v4 = E(nc.semaphore("s_v4"))
        s_t2 = E(nc.semaphore("s_t2"))
        s_sc1 = E(nc.semaphore("s_sc1"))
        s_v5 = E(nc.semaphore("s_v5"))
        s_sp = [E(nc.semaphore(f"s_sp{i}")) for i in range(2)]
        with ExitStack() as ph:
            PE = ph.enter_context
            xg = [PE(nc.sbuf_tensor(f"xg{i}", [128, DT, GSZ], F32)) for i in range(2)]
            xb = [PE(nc.sbuf_tensor(f"xb{i}", [128, DT, GSZ], BF16)) for i in range(2)]
            sq = [PE(nc.sbuf_tensor(f"sq{i}", [128, DT, GSZ], BF16)) for i in range(2)]
            xn = [PE(nc.sbuf_tensor(f"xn{i}", [128, DT, GSZ], BF16)) for i in range(2)]
            sb_s = [PE(nc.sbuf_tensor(f"sb_s{i}", [128, 1, GSZ], BF16)) for i in range(2)]
            sb_ms = [PE(nc.sbuf_tensor(f"sb_ms{i}", [128, 1, GSZ], BF16)) for i in range(2)]
            m_row = PE(nc.sbuf_tensor("m_row", [1, GSZ], F32))
            ex2_row = PE(nc.sbuf_tensor("ex2_row", [1, GSZ], F32))
            m2_row = PE(nc.sbuf_tensor("m2_row", [1, GSZ], F32))
            var_row = PE(nc.sbuf_tensor("var_row", [1, GSZ], F32))
            ln_row = PE(nc.sbuf_tensor("ln_row", [1, GSZ], F32))
            rstd_row = PE(nc.sbuf_tensor("rstd_row", [1, GSZ], F32))
            ms_row = PE(nc.sbuf_tensor("ms_row", [1, GSZ], F32))
            ps_sx = PE(nc.psum_tensor("ps_sx", [1, GSZ], F32))
            ps_sx2 = PE(nc.psum_tensor("ps_sx2", [1, GSZ], F32))
            ps_b1 = PE(nc.psum_tensor("ps_b1", [128, GSZ], F32))
            ps_b2 = PE(nc.psum_tensor("ps_b2", [128, GSZ], F32))
            with nc.Block() as blk:
                @blk.sync
                def _(s):
                    phase_wait(s)
                    for g in range(NG):
                        if g >= 2:
                            s.wait_ge(s_v1, g - 1)
                            s.wait_ge(s_sq1, g - 1)
                            s.wait_ge(s_x[g % 2], 16 * (g // 2))
                        s.dma_start(
                            xg[g % 2][:], xT_r[:, :, g * GSZ:(g + 1) * GSZ]
                        ).then_inc(s_x[g % 2], 16)
                    phase_inc(s)

                @blk.vector
                def _(v):
                    phase_wait(v)
                    for g in range(NG + 1):
                        if g < NG:
                            v.wait_ge(s_x[g % 2], 16 * (g // 2 + 1))
                            if g >= 2:
                                v.wait_ge(s_t1, g - 1)
                            v.tensor_copy(xb[g % 2][:], xg[g % 2][:]).then_inc(s_v1, 1)
                        if g >= 1:
                            p = g - 1
                            v.wait_ge(s_t1, p + 1)
                            na = vedge(v.tensor_scalar_mul(m_row[:], ps_sx[:], 1.0 / D))
                            v.tensor_scalar_mul(ex2_row[:], ps_sx2[:],
                                                1.0 / D).then_inc(s_v2, 1)
                            v.wait_ge(vSelf, na)
                            n1 = vedge(v.tensor_mul(m2_row[:], m_row[:], m_row[:]))
                            v.wait_ge(s_v2, p + 1)
                            v.wait_ge(vSelf, n1)
                            v.tensor_sub(var_row[:], ex2_row[:], m2_row[:]).then_inc(s_v3, 1)
                            v.wait_ge(s_sc0, p + 1)
                            v.tensor_mul(ms_row[:], m_row[:], rstd_row[:]).then_inc(s_v4, 1)
                            v.wait_ge(s_sc1, p + 1)
                            if p >= 2:
                                v.wait_ge(s_sp[p % 2], 16 * (p // 2))
                            a1, b1 = broadcast_tensor_aps(xb[p % 2][:], sb_s[p % 2][:])
                            n2 = vedge(v.tensor_mul(xn[p % 2][:], a1, b1))
                            v.wait_ge(vSelf, n2)
                            a2, b2 = broadcast_tensor_aps(xn[p % 2][:], sb_ms[p % 2][:])
                            v.tensor_sub(xn[p % 2][:], a2, b2).then_inc(s_v5, 1)
                    phase_inc(v)

                @blk.scalar
                def _(sc):
                    phase_wait(sc)
                    for g in range(NG + 1):
                        if g < NG:
                            sc.wait_ge(s_x[g % 2], 16 * (g // 2 + 1))
                            if g >= 2:
                                sc.wait_ge(s_t1, g - 1)
                            sc.activation(sq[g % 2][:], xg[g % 2][:],
                                          AF.Square).then_inc(s_sq1, 1)
                        if g >= 1:
                            p = g - 1
                            sc.wait_ge(s_v3, p + 1)
                            if p >= 1:
                                sc.wait_ge(s_t2, p)
                            m1 = sedge(sc.activation(ln_row[:], var_row[:], AF.Ln,
                                                     bias=cst_sb[0:1, 0:1]))
                            sc.wait_ge(sSelf, m1)
                            sc.activation(rstd_row[:], ln_row[:], AF.Exp,
                                          scale=-0.5).then_inc(s_sc0, 1)
                            sc.wait_ge(s_t2, p + 1)
                            sc.copy(sb_s[p % 2][:, 0, :], ps_b1[:])
                            sc.copy(sb_ms[p % 2][:, 0, :], ps_b2[:]).then_inc(s_sc1, 1)
                    phase_inc(sc)

                @blk.tensor
                def _(t):
                    phase_wait(t)
                    for g in range(NG + 1):
                        if g < NG:
                            t.wait_ge(s_v1, g + 1)
                            t.wait_ge(s_sq1, g + 1)
                            if g >= 1:
                                t.wait_ge(s_v2, g)
                            for dt in range(DT):
                                t.matmul(ps_sx[:], onec_sb[:], xb[g % 2][:, dt, :],
                                         start=(dt == 0), stop=(dt == DT - 1))
                            last = None
                            for dt in range(DT):
                                last = t.matmul(ps_sx2[:], onec_sb[:], sq[g % 2][:, dt, :],
                                                start=(dt == 0), stop=(dt == DT - 1))
                            last.then_inc(s_t1, 1)
                        if g >= 1:
                            p = g - 1
                            t.wait_ge(s_v4, p + 1)
                            if p >= 1:
                                t.wait_ge(s_sc1, p)
                            t.matmul(ps_b1[:], onerow_sb[:], rstd_row[:], start=True, stop=True)
                            t.matmul(ps_b2[:], onerow_sb[:], ms_row[:],
                                     start=True, stop=True).then_inc(s_t2, 1)
                    phase_inc(t)

                @blk.gpsimd
                def _(g_):
                    phase_wait(g_)
                    for g in range(NG):
                        g_.wait_ge(s_v5, g + 1)
                        if g >= 2:
                            g_.wait_ge(s_sp[g % 2], 16 * (g // 2))
                        g_.dma_start(
                            xnT_r[:, :, g * GSZ:(g + 1) * GSZ], xn[g % 2][:]
                        ).then_inc(s_sp[g % 2], 16)
                    g_.wait_ge(s_sp[0], 16 * (NG // 2))
                    g_.wait_ge(s_sp[1], 16 * (NG // 2))
                    phase_inc(g_)
        PH["k"] += 1

        jobs = [(w, 0) for w in range(JW)] + [(w, 1) for w in range(32, JW)]
        bd_jobs = [i for i, (w, h) in enumerate(jobs) if w < 40]
        bd_rank = {j: r for r, j in enumerate(bd_jobs)}
        first_k_rank = min(r for r, j in enumerate(bd_jobs) if jobs[j][0] >= 32)
        s_xh = E(nc.semaphore("s_xh"))
        s_xh2 = E(nc.semaphore("s_xh2"))
        s_w2 = [E(nc.semaphore(f"s_w2{i}")) for i in range(3)]
        s_jt2 = E(nc.semaphore("s_jt2"))
        s_ev2 = E(nc.semaphore("s_ev2"))
        s_st2 = [E(nc.semaphore(f"s_st2{i}")) for i in range(2)]
        s_sqst = E(nc.semaphore("s_sqst"))
        s_bnd4 = E(nc.semaphore("s_bnd4"))
        s_bdr = E(nc.semaphore("s_bdr"))
        s_nbB = E(nc.semaphore("s_nbB"))
        with ExitStack() as ph:
            PE = ph.enter_context
            xnh = PE(nc.sbuf_tensor("xnh", [128, DT, NOWN], BF16))
            wcol = [PE(nc.sbuf_tensor(f"wcol{i}", [128, DT, 128], BF16)) for i in range(3)]
            stage = [PE(nc.sbuf_tensor(f"stage{i}", [128, NOWN], BF16)) for i in range(2)]
            sqst = PE(nc.sbuf_tensor("sqst", [128, NOWN], BF16))
            qmax_r = PE(nc.sbuf_tensor("qmax_r", [1, 1], F32))
            kmax_r = PE(nc.sbuf_tensor("kmax_r", [1, 1], F32))
            blkr = PE(nc.sbuf_tensor("blkr", [1, 1], F32))
            bbr = PE(nc.sbuf_tensor("bbr", [1, 1], F32))
            ps_q = [PE(nc.psum_tensor(f"ps_q{i}", [128, 2048], F32)) for i in range(2)]

            def p2_dest(w, half):
                cols = slice(half * 2048, (half + 1) * 2048)
                if w < 32:
                    return q_scr[w * 128:(w + 1) * 128, :]
                if w < 40:
                    return k_scr[(w - 32) * 128:(w - 31) * 128, cols]
                return v_scr[(w - 40) * 128:(w - 39) * 128, cols]

            with nc.Block() as blk:
                @blk.sync
                def _(s):
                    phase_wait(s)
                    for q4 in range(4):
                        s.dma_start(
                            xnh[:, :, q4 * 512:(q4 + 1) * 512],
                            xnT_r[:, :, q4 * 512:(q4 + 1) * 512],
                        ).then_inc(s_xh, 16)
                    s.wait_ge(s_jt2, JW)
                    s.wait_ge(s_xh, 64)
                    for q4 in range(4):
                        s.dma_start(
                            xnh[:, :, q4 * 512:(q4 + 1) * 512],
                            xnT_r[:, :, 2048 + q4 * 512:2048 + (q4 + 1) * 512],
                        ).then_inc(s_xh2, 16)
                    phase_inc(s)

                @blk.gpsimd
                def _(g):
                    phase_wait(g)
                    for idx, (w, half) in enumerate(jobs):
                        if idx >= 3:
                            g.wait_ge(s_jt2, idx - 2)
                            g.wait_ge(s_w2[idx % 3], 16 * (idx // 3))
                        g.dma_start(
                            wcol[idx % 3][:], wqT_r[:, :, w * 128:(w + 1) * 128]
                        ).then_inc(s_w2[idx % 3], 16)
                    phase_inc(g)

                @blk.tensor
                def _(t):
                    phase_wait(t)
                    for idx, (w, half) in enumerate(jobs):
                        t.wait_ge(s_w2[idx % 3], 16 * (idx // 3 + 1))
                        if idx < JW:
                            t.wait_ge(s_xh, 64)
                        else:
                            t.wait_ge(s_xh2, 64)
                        if idx >= 2:
                            t.wait_ge(s_ev2, idx - 1)
                            nred = sum(1 for j in bd_jobs if j <= idx - 2)
                            if nred:
                                t.wait_ge(s_bdr, nred)
                        last = None
                        for dt in range(DT):
                            for nb in range(4):
                                last = t.matmul(
                                    ps_q[idx % 2][:, nb * 512:(nb + 1) * 512],
                                    wcol[idx % 3][:, dt, :],
                                    xnh[:, dt, nb * 512:(nb + 1) * 512],
                                    start=(dt == 0), stop=(dt == DT - 1),
                                )
                        last.then_inc(s_jt2, 1)
                        pj = idx - 1
                        if pj in bd_rank:
                            r = bd_rank[pj]
                            t.wait_ge(s_ev2, pj + 1)
                            t.wait_ge(s_sqst, r + 1)
                            for nb in range(4):
                                t.matmul(ps_q[pj % 2][0:1, 0:512], onec_sb[:],
                                         sqst[:, nb * 512:(nb + 1) * 512],
                                         start=(nb == 0),
                                         stop=(nb == 3)).then_inc(s_bnd4, 1)
                    pj = len(jobs) - 1
                    if pj in bd_rank:
                        r = bd_rank[pj]
                        t.wait_ge(s_ev2, pj + 1)
                        t.wait_ge(s_sqst, r + 1)
                        for nb in range(4):
                            t.matmul(ps_q[pj % 2][0:1, 0:512], onec_sb[:],
                                     sqst[:, nb * 512:(nb + 1) * 512],
                                     start=(nb == 0), stop=(nb == 3)).then_inc(s_bnd4, 1)
                    t.wait_ge(s_ev2, len(jobs))
                    t.wait_ge(s_bdr, len(bd_jobs))
                    t.wait_ge(s_nbB, 3)
                    t.matmul(ps_q[0][:, 0:1], onerow_sb[:], bbr[:],
                             start=True, stop=True).then_inc(s_nbB, 1)
                    phase_inc(t)

                @blk.scalar
                def _(sc):
                    phase_wait(sc)
                    for idx, (w, half) in enumerate(jobs):
                        sc.wait_ge(s_jt2, idx + 1)
                        if idx >= 2:
                            sc.wait_ge(s_st2[idx % 2], 16 * (idx // 2))
                            nsq = sum(1 for j in bd_jobs if j <= idx - 2)
                            if nsq:
                                sc.wait_ge(s_sqst, nsq)
                        last = None
                        for nb in range(4):
                            last = sc.activation(
                                stage[idx % 2][:, nb * 512:(nb + 1) * 512],
                                ps_q[idx % 2][:, nb * 512:(nb + 1) * 512],
                                AF.Identity, bias=bqkv_sb[:, w:w + 1], scale=1.0,
                            )
                        last.then_inc(s_ev2, 1)
                        sc.wait_ge(s_ev2, idx + 1)
                        sc.dma_start(p2_dest(w, half),
                                     stage[idx % 2][:]).then_inc(s_st2[idx % 2], 16)
                    sc.wait_ge(s_st2[0], 16 * ((len(jobs) + 1) // 2))
                    sc.wait_ge(s_st2[1], 16 * (len(jobs) // 2))
                    sc.wait_ge(s_nbB, 1)
                    m1 = sedge(sc.activation(bbr[:], bbr[:], AF.Ln))
                    sc.wait_ge(sSelf, m1)
                    sc.activation(bbr[:], bbr[:], AF.Exp, scale=0.5,
                                  bias=cst_sb[0:1, 1:2]).then_inc(s_nbB, 1)
                    sc.wait_ge(s_nbB, 4)
                    sc.copy(negB_sb[:], ps_q[0][:, 0:1])
                    phase_inc(sc)

                @blk.vector
                def _(v):
                    phase_wait(v)
                    for r, j in enumerate(bd_jobs):
                        w, half = jobs[j]
                        v.wait_ge(s_ev2, j + 1)
                        if r >= 1:
                            v.wait_ge(s_bnd4, 4 * r)
                        v.tensor_mul(sqst[:], stage[j % 2][:],
                                     stage[j % 2][:]).then_inc(s_sqst, 1)
                        v.wait_ge(s_bnd4, 4 * (r + 1))
                        n1 = vedge(v.reduce_max(blkr[:], ps_q[j % 2][0:1, 0:512], axis=AX))
                        v.wait_ge(vSelf, n1)
                        if r == 0:
                            v.tensor_copy(qmax_r[:], blkr[:]).then_inc(s_bdr, 1)
                        elif r == first_k_rank:
                            v.tensor_copy(kmax_r[:], blkr[:]).then_inc(s_bdr, 1)
                        elif w < 32:
                            v.tensor_max(qmax_r[:], qmax_r[:], blkr[:]).then_inc(s_bdr, 1)
                        else:
                            v.tensor_max(kmax_r[:], kmax_r[:], blkr[:]).then_inc(s_bdr, 1)
                    v.wait_ge(s_bdr, len(bd_jobs))
                    v.tensor_mul(bbr[:], qmax_r[:], kmax_r[:]).then_inc(s_nbB, 1)
                    v.wait_ge(s_nbB, 2)
                    v.tensor_scalar_mul(bbr[:], bbr[:], -1.0).then_inc(s_nbB, 1)
                    phase_inc(v)
        PH["k"] += 1

        s_xn5 = [E(nc.semaphore(f"s_xn5{i}")) for i in range(2)]
        s_gw5 = E(nc.semaphore("s_gw5"))
        s_lg = E(nc.semaphore("s_lg"))
        s_vm = E(nc.semaphore("s_vm"))
        s_e8 = E(nc.semaphore("s_e8"))
        s_rt = E(nc.semaphore("s_rt"))
        s_tr = E(nc.semaphore("s_tr"))
        s_rc = E(nc.semaphore("s_rc"))
        s_sl5 = [E(nc.semaphore(f"s_sl5{i}")) for i in range(2)]
        s_im = E(nc.semaphore("s_im"))
        s_is5 = E(nc.semaphore("s_is5"))
        s_xw5 = [E(nc.semaphore(f"s_xw5{i}")) for i in range(2)]
        s_vl5 = [E(nc.semaphore(f"s_vl5{i}")) for i in range(2)]
        s_vt5 = E(nc.semaphore("s_vt5"))
        s_vc5 = E(nc.semaphore("s_vc5"))
        s_vw5 = [E(nc.semaphore(f"s_vw5{i}")) for i in range(2)]
        with ExitStack() as ph:
            PE = ph.enter_context
            vband = [PE(nc.sbuf_tensor(f"vband{i}", [128, S], BF16)) for i in range(2)]
            vstage = [PE(nc.sbuf_tensor(f"vstage{i}", [128, DT, 128], BF16)) for i in range(2)]
            ps_vt = [PE(nc.psum_tensor(f"ps_vt{i}", [128, 128], BF16)) for i in range(2)]
            gw_sb = PE(nc.sbuf_tensor("gw_sb", [128, DT, 8], BF16))
            ics_sb = PE(nc.sbuf_tensor("ics_sb", [8, D], BF16))
            xn_t = [PE(nc.sbuf_tensor(f"xn_t{i}", [128, DT, 128], BF16)) for i in range(2)]
            e8 = PE(nc.sbuf_tensor("e8", [128, 8], F32))
            negmx = PE(nc.sbuf_tensor("negmx", [128, 1], F32))
            sm8 = PE(nc.sbuf_tensor("sm8", [128, 1], F32))
            rs8 = PE(nc.sbuf_tensor("rs8", [128, 1], F32))
            rout_bf = PE(nc.sbuf_tensor("rout_bf", [128, 8], BF16))
            xn_sl = [PE(nc.sbuf_tensor(f"xn_sl{i}", [128, DT, 512], BF16)) for i in range(2)]
            xs_sl = [PE(nc.sbuf_tensor(f"xs_sl{i}", [128, DT, 512], BF16)) for i in range(2)]
            ps_l = PE(nc.psum_tensor("ps_l", [128, 8], F32))
            ps_rt = PE(nc.psum_tensor("ps_rt", [8, 128], BF16))
            ps_i = [PE(nc.psum_tensor(f"ps_i{i}", [128, 512], F32)) for i in range(2)]
            with nc.Block() as blk:
                @blk.sync
                def _(s):
                    phase_wait(s)
                    for nt in range(NCH):
                        if nt >= 2:
                            s.wait_ge(s_lg, nt - 1)
                            s.wait_ge(s_xn5[nt % 2], 16 * (nt // 2))
                        s.dma_start(
                            xn_t[nt % 2][:], xnT_r[:, :, nt * 128:(nt + 1) * 128]
                        ).then_inc(s_xn5[nt % 2], 16)
                        if nt % 2 == 0 and nt // 2 < NKV:
                            g = nt // 2
                            if g >= 2:
                                s.wait_ge(s_vt5, DT * (g - 1))
                                s.wait_ge(s_vl5[g % 2], 16 * (g // 2))
                            s.dma_start(
                                vband[g % 2][:], v_scr[g * 128:(g + 1) * 128, :]
                            ).then_inc(s_vl5[g % 2], 16)
                    for nb in range(4):
                        if nb >= 2:
                            s.wait_ge(s_is5, DT * (nb - 1))
                            s.wait_ge(s_sl5[nb % 2], 16 * (nb // 2))
                        s.dma_start(
                            xn_sl[nb % 2][:], xnT_r[:, :, nb * 512:(nb + 1) * 512]
                        ).then_inc(s_sl5[nb % 2], 16)
                    phase_inc(s)

                @blk.gpsimd
                def _(g):
                    phase_wait(g)
                    g.dma_start(gw_sb[:], gwT_r[:]).then_inc(s_gw5, 16)
                    g.dma_start(ics_sb[:], ics8).then_inc(s_gw5, 16)
                    for gg in range(NKV):
                        g.wait_ge(s_vc5, DT * (gg + 1))
                        if gg >= 2:
                            g.wait_ge(s_vw5[gg % 2], 16 * (gg // 2))
                        g.dma_start(
                            v_tok_scr[gg * S:(gg + 1) * S, :].rearrange(
                                "(kt p) d -> p kt d", p=128),
                            vstage[gg % 2][:],
                        ).then_inc(s_vw5[gg % 2], 16)
                    for nb in range(4):
                        g.wait_ge(s_is5, DT * (nb + 1))
                        if nb >= 2:
                            g.wait_ge(s_xw5[nb % 2], 16 * (nb // 2))
                        g.dma_start(
                            xsT_r[:, :, nb * 512:(nb + 1) * 512], xs_sl[nb % 2][:]
                        ).then_inc(s_xw5[nb % 2], 16)
                    g.wait_ge(s_xw5[0], 32)
                    g.wait_ge(s_xw5[1], 32)
                    g.wait_ge(s_vw5[0], 16 * (NKV // 2))
                    g.wait_ge(s_vw5[1], 16 * (NKV // 2))
                    phase_inc(g)

                @blk.tensor
                def _(t):
                    phase_wait(t)
                    t.wait_ge(s_gw5, 32)
                    vt_jobs = [(g, kt) for g in range(NKV) for kt in range(DT)]

                    def vt_op():
                        if not vt_jobs:
                            return
                        g, kt = vt_jobs.pop(0)
                        gi = g * DT + kt
                        if kt == 0:
                            t.wait_ge(s_vl5[g % 2], 16 * (g // 2 + 1))
                        if gi >= 2:
                            t.wait_ge(s_vc5, gi - 1)
                        t.transpose(ps_vt[gi % 2][:], vband[g % 2][:, kt * 128:(kt + 1) * 128],
                                    iden_sb[:]).then_inc(s_vt5, 1)

                    for nt in range(NCH):
                        t.wait_ge(s_xn5[nt % 2], 16 * (nt // 2 + 1))
                        if nt >= 1:
                            t.wait_ge(s_e8, nt)
                        last = None
                        for dt in range(DT):
                            last = t.matmul(ps_l[:], xn_t[nt % 2][:, dt, :], gw_sb[:, dt, :],
                                            start=(dt == 0), stop=(dt == DT - 1))
                        last.then_inc(s_lg, 1)
                        for _ in range(16):
                            vt_op()
                        t.wait_ge(s_rt, nt + 1)
                        if nt >= 1:
                            t.wait_ge(s_rc, nt)
                        t.transpose(ps_rt[:], rout_bf[:], iden_sb[:]).then_inc(s_tr, 1)
                    t.wait_ge(s_rc, NCH)
                    for nb in range(4):
                        t.wait_ge(s_sl5[nb % 2], 16 * (nb // 2 + 1))
                        for dt in range(DT):
                            k = nb * DT + dt
                            if k >= 2:
                                t.wait_ge(s_is5, k - 1)
                            t.matmul(ps_i[k % 2][:], ics_sb[:, dt * 128:(dt + 1) * 128],
                                     routT_sb[:, nb * 512:(nb + 1) * 512],
                                     start=True, stop=True).then_inc(s_im, 1)
                    while vt_jobs:
                        vt_op()
                    phase_inc(t)

                @blk.vector
                def _(v):
                    phase_wait(v)
                    for nt in range(NCH):
                        v.wait_ge(s_lg, nt + 1)
                        if nt >= 1:
                            v.wait_ge(s_e8, nt)
                        v.reduce_max(negmx[:], ps_l[:], axis=AX, negate=True).then_inc(s_vm, 1)
                        v.wait_ge(s_e8, nt + 1)
                        if nt >= 1:
                            v.wait_ge(s_tr, nt)
                        n1 = vedge(v.reduce_sum(sm8[:], e8[:], axis=AX))
                        v.wait_ge(vSelf, n1)
                        n2 = vedge(v.reciprocal(rs8[:], sm8[:]))
                        v.wait_ge(vSelf, n2)
                        v.tensor_scalar_mul(rout_bf[:], e8[:], rs8[:]).then_inc(s_rt, 1)
                    for nb in range(4):
                        for dt in range(DT):
                            k = nb * DT + dt
                            v.wait_ge(s_im, k + 1)
                            if nb >= 2 and dt == 0:
                                v.wait_ge(s_xw5[nb % 2], 16 * (nb // 2))
                            v.tensor_mul(xs_sl[nb % 2][:, dt, :], xn_sl[nb % 2][:, dt, :],
                                         ps_i[k % 2][:]).then_inc(s_is5, 1)
                    phase_inc(v)

                @blk.scalar
                def _(sc):
                    phase_wait(sc)
                    vc_jobs = [(g, kt) for g in range(NKV) for kt in range(DT)]

                    def vc_op():
                        if not vc_jobs:
                            return
                        g, kt = vc_jobs.pop(0)
                        gi = g * DT + kt
                        sc.wait_ge(s_vt5, gi + 1)
                        if g >= 2 and kt == 0:
                            sc.wait_ge(s_vw5[g % 2], 16 * (g // 2))
                        sc.copy(vstage[g % 2][:, kt, :], ps_vt[gi % 2][:]).then_inc(s_vc5, 1)

                    for nt in range(NCH):
                        sc.wait_ge(s_vm, nt + 1)
                        if nt >= 1:
                            sc.wait_ge(s_rt, nt)
                        sc.activation(e8[:], ps_l[:], AF.Exp, bias=negmx[:],
                                      scale=1.0).then_inc(s_e8, 1)
                        for _ in range(16):
                            vc_op()
                        sc.wait_ge(s_tr, nt + 1)
                        sc.copy(routT_sb[:, nt * 128:(nt + 1) * 128], ps_rt[:]).then_inc(s_rc, 1)
                    while vc_jobs:
                        vc_op()
                    phase_inc(sc)
        PH["k"] += 1

        s_c3 = E(nc.semaphore("s_c3"))
        s_graw = E(nc.semaphore("s_graw"))
        s_qkraw = E(nc.semaphore("s_qkraw"))
        s_vtk = [E(nc.semaphore(f"s_vtk{i}")) for i in range(2)]
        s_S = E(nc.semaphore("s_S"))
        s_P = E(nc.semaphore("s_P"))
        s_qbD = E(nc.semaphore("s_qbD"))
        s_avc = E(nc.semaphore("s_avc"))
        s_r3 = E(nc.semaphore("s_r3"))
        s_rb = E(nc.semaphore("s_rb"))
        s_N = E(nc.semaphore("s_N"))
        s_og = E(nc.semaphore("s_og"))
        TRI_G = sum(2 * (i + 1) for i in range(NCH))
        with ExitStack() as ph:
            PE = ph.enter_context
            cost_sb = PE(nc.sbuf_tensor("cost_sb", [128, S], F32))
            sint_sb = PE(nc.sbuf_tensor("sint_sb", [128, S], F32))
            mtri_sb = PE(nc.sbuf_tensor("mtri_sb", [128, 512], BF16))
            moth_sb = PE(nc.sbuf_tensor("moth_sb", [128, 512], BF16))
            kraw = PE(nc.sbuf_tensor("kraw", [128, S], BF16))
            kswap = PE(nc.sbuf_tensor("kswap", [128, S], BF16))
            qraw = PE(nc.sbuf_tensor("qraw", [128, 4, NOWN], BF16))
            qswap = PE(nc.sbuf_tensor("qswap", [128, 4, NOWN], BF16))
            krope = [PE(nc.sbuf_tensor(f"krope{i}", [128, S], BF16)) for i in range(2)]
            qrope = [PE(nc.sbuf_tensor(f"qrope{i}", [128, NCH, 4, 128], BF16)) for i in range(2)]
            vtok = [PE(nc.sbuf_tensor(f"vtok{i}", [128, DT, 128], BF16)) for i in range(2)]
            outg = [PE(nc.sbuf_tensor(f"outg{i}", [128, NCH, 4, 128], BF16)) for i in range(2)]
            pt = PE(nc.sbuf_tensor("pt", [128, 6, 512], BF16))
            t1b = [PE(nc.sbuf_tensor(f"t1b{i}", [128, 512], BF16)) for i in range(2)]
            av_sb = [PE(nc.sbuf_tensor(f"av_sb{i}", [128, 512], F32)) for i in range(2)]
            dr_bf = [PE(nc.sbuf_tensor(f"dr_bf{i}", [1, 512], BF16)) for i in range(2)]
            dr_ln = PE(nc.sbuf_tensor("dr_ln", [1, 512], F32))
            ps_s = PE(nc.psum_tensor("ps_s", [128, 2048], F32))
            ps_av = [PE(nc.psum_tensor(f"ps_av{i}", [128, 512], F32)) for i in range(2)]
            ps_d = [PE(nc.psum_tensor(f"ps_d{i}", [1, 512], F32)) for i in range(2)]

            def kt_list(i):
                return [("own", j) for j in range(i + 1)] + [("oth", j) for j in range(i + 1)]

            def rope_srcs(k):
                if k < 8:
                    cols = slice(k * 512, (k + 1) * 512)
                    return kraw[:, cols], kswap[:, cols], cols
                h, bq = divmod(k - 8, 4)
                cols = slice(bq * 512, (bq + 1) * 512)
                return qraw[:, h, cols], qswap[:, h, cols], cols

            def rope_dst(g_, k):
                if k < 8:
                    return krope[g_ % 2][:, k * 512:(k + 1) * 512]
                h, bq = divmod(k - 8, 4)
                return qrope[g_ % 2][:, 4 * bq:4 * bq + 4, h, :]

            with nc.Block() as blk:
                @blk.sync
                def _(s):
                    phase_wait(s)
                    s.dma_start(cost_sb[:], cost).then_inc(s_c3, 16)
                    s.dma_start(sint_sb[:], sint).then_inc(s_c3, 16)
                    s.dma_start(mtri_sb[:], mtri).then_inc(s_c3, 16)
                    s.dma_start(moth_sb[:], moth).then_inc(s_c3, 16)
                    for g in range(NKV):
                        if g >= 2:
                            s.wait_ge(s_N, (g - 1) * NCH)
                            s.wait_ge(s_vtk[g % 2], 16 * (g // 2))
                        s.dma_start(vtok[g % 2][:], vtok_r[g]).then_inc(s_vtk[g % 2], 16)
                    phase_inc(s)

                @blk.gpsimd
                def _(g_):
                    phase_wait(g_)
                    for g in range(NKV):
                        if g >= 1:
                            g_.wait_ge(s_qkraw, g)
                            g_.wait_ge(s_graw, 96 * g)
                        g_.dma_start(kraw[:], k_scr[g * 128:(g + 1) * 128, :]).then_inc(s_graw, 16)
                        g_.dma_start(qraw[:], q_scr_r[:, 4 * g:4 * g + 4, :]).then_inc(s_graw, 16)
                        g_.dma_start(kswap[0:64, :],
                                     k_scr[g * 128 + 64:(g + 1) * 128, :]).then_inc(s_graw, 16)
                        g_.dma_start(kswap[64:128, :],
                                     k_scr[g * 128:g * 128 + 64, :]).then_inc(s_graw, 16)
                        g_.dma_start(qswap[0:64, :, :],
                                     q_scr_r[64:128, 4 * g:4 * g + 4, :]).then_inc(s_graw, 16)
                        g_.dma_start(qswap[64:128, :, :],
                                     q_scr_r[0:64, 4 * g:4 * g + 4, :]).then_inc(s_graw, 16)
                        if g >= 1:
                            g_.wait_ge(s_N, g * NCH)
                            g_.wait_ge(s_og, 64 * (g - 1))
                            for h in range(4):
                                g_.dma_start(
                                    outT_r[:, 4 * (g - 1) + h, :].rearrange(
                                        "p (a b) -> p a b", b=128),
                                    outg[(g - 1) % 2][:, :, h, :],
                                ).then_inc(s_og, 16)
                    g_.wait_ge(s_N, NKV * NCH)
                    g_.wait_ge(s_og, 64 * (NKV - 1))
                    for h in range(4):
                        g_.dma_start(
                            outT_r[:, 4 * (NKV - 1) + h, :].rearrange(
                                "p (a b) -> p a b", b=128),
                            outg[(NKV - 1) % 2][:, :, h, :],
                        ).then_inc(s_og, 16)
                    g_.wait_ge(s_og, 16 * 4 * NKV)
                    phase_inc(g_)

                @blk.tensor
                def _(t):
                    phase_wait(t)
                    t.wait_ge(s_c3, 64)

                    deferred = []
                    avd_pend = []
                    tglob = 0
                    for g in range(NKV):
                        t.wait_ge(s_qkraw, g + 1)
                        t.wait_ge(s_vtk[g % 2], 16 * (g // 2 + 1))
                        for i in range(NCH):
                            Q = g * NCH + i
                            kts = kt_list(i)
                            ntr = len(kts)

                            def mk_avd(pidx, base=tglob, Q=Q, g=g, kts=kts, ntr=ntr):
                                def op():
                                    side, ch = kts[pidx]
                                    vslot = ch if side == "own" else 16 + ch
                                    pfirst = pidx == 0
                                    plast = pidx == ntr - 1
                                    tc = base + pidx
                                    t.wait_ge(s_P, tc // 2 + 1)
                                    if pfirst and Q >= 2:
                                        t.wait_ge(s_N, Q - 1)
                                    t.matmul(ps_av[Q % 2][:], vtok[g % 2][:, vslot, :],
                                             pt[:, tc % 6, :], start=pfirst, stop=plast)
                                    if pidx % 2 == 1 or plast:
                                        for pp in ([pidx - 1, pidx] if pidx % 2 == 1
                                                   else [pidx]):
                                            tc2 = base + pp
                                            mmd = t.matmul(ps_d[Q % 2][:], onec_sb[:],
                                                           pt[:, tc2 % 6, :],
                                                           start=(pp == 0),
                                                           stop=(pp == ntr - 1))
                                            if pp == ntr - 1:
                                                mmd.then_inc(s_qbD, 1)
                                return op

                            for j, (side, ch) in enumerate(kts):
                                tcur = tglob + j
                                masked = (j == i) or (j == ntr - 1)
                                if tcur >= 4:
                                    t.wait_ge(s_P, tcur // 2 - 1)
                                kcols = ch * 128 if side == "own" else 2048 + ch * 128
                                mm = t.matmul(
                                    ps_s[:, (tcur % 4) * 512:(tcur % 4 + 1) * 512],
                                    krope[g % 2][:, kcols:kcols + 128],
                                    qrope[g % 2][:, i, :, :].rearrange("p a b -> p (a b)"),
                                    start=True, stop=(not masked),
                                )
                                if masked:
                                    mk = mtri_sb if j == i else moth_sb
                                    mm = t.matmul(ps_s[:, (tcur % 4) * 512:(tcur % 4 + 1) * 512],
                                                  iden_sb[:], mk[:], start=False, stop=True)
                                mm.then_inc(s_S, 1)
                                if j == 5 and deferred:
                                    deferred.pop(0)()
                                avd_pend.append(mk_avd(j))
                                if len(avd_pend) > 4:
                                    avd_pend.pop(0)()
                            if i <= 2:
                                while avd_pend:
                                    avd_pend.pop(0)()
                            tglob += ntr

                            def rb_op(Q=Q):
                                t.wait_ge(s_avc, Q + 1)
                                t.wait_ge(s_r3, Q + 1)
                                t.matmul(ps_av[Q % 2][:], onerow_bf_sb[:], dr_bf[Q % 2][:],
                                         start=True, stop=True).then_inc(s_rb, 1)
                            deferred.append(rb_op)
                            if i <= 2:
                                deferred.pop(0)()
                        while avd_pend:
                            avd_pend.pop(0)()
                        while deferred:
                            deferred.pop(0)()
                    phase_inc(t)

                @blk.scalar
                def _(sc):
                    phase_wait(sc)
                    tglob = 0
                    pend_av = None
                    for g in range(NKV):
                        for i in range(NCH):
                            Q = g * NCH + i
                            ntr = 2 * (i + 1)
                            for j in range(0, ntr, 2):
                                tcur = tglob + j
                                sc.wait_ge(s_S, tcur + 2)
                                half = (tcur % 4) // 2
                                slot = tcur % 6
                                sc.activation(
                                    pt[:, slot:slot + 2, :].rearrange("p a b -> p (a b)"),
                                    ps_s[:, half * 1024:(half + 1) * 1024],
                                    AF.Exp, bias=negB_sb[:], scale=ISQ,
                                ).then_inc(s_P, 1)
                                if j == 0 and pend_av is not None:
                                    pend_av()
                                    pend_av = None
                            tglob += ntr

                            def av_op(Q=Q):
                                sc.wait_ge(s_qbD, Q + 1)
                                sc.copy(av_sb[Q % 2][:], ps_av[Q % 2][:]).then_inc(s_avc, 1)
                                m1 = sedge(sc.activation(dr_ln[:], ps_d[Q % 2][:], AF.Ln,
                                                         scale=2.0 ** 64))
                                sc.wait_ge(sSelf, m1)
                                sc.activation(dr_bf[Q % 2][:], dr_ln[:], AF.Exp,
                                              scale=-1.0,
                                              bias=cst_sb[0:1, 2:3]).then_inc(s_r3, 1)
                            if i <= 2:
                                av_op()
                            else:
                                pend_av = av_op
                        if pend_av is not None:
                            pend_av()
                            pend_av = None
                    phase_inc(sc)

                @blk.vector
                def _(v):
                    phase_wait(v)
                    v.wait_ge(s_c3, 64)

                    def mk_rope(g_, k):
                        def op():
                            raw, swp, cols = rope_srcs(k)
                            dst = rope_dst(g_, k)
                            v.wait_ge(s_graw, 96 * (g_ + 1))
                            if g_ >= 2:
                                v.wait_ge(s_S, TRI_G * (g_ - 1))
                            n1 = vedge(v.tensor_mul(t1b[k % 2][:], swp, sint_sb[:, cols]))
                            if len(dst.shape) == 3:
                                n2 = vedge(v.tensor_mul(
                                    dst, raw.rearrange("p (a b) -> p a b", b=128),
                                    cost_sb[:, cols].rearrange("p (a b) -> p a b", b=128)))
                                v.wait_ge(vSelf, n2)
                                v.wait_ge(vSelf, n1)
                                mm = v.tensor_add(
                                    dst, dst,
                                    t1b[k % 2][:].rearrange("p (a b) -> p a b", b=128))
                            else:
                                n2 = vedge(v.tensor_mul(dst, raw, cost_sb[:, cols]))
                                v.wait_ge(vSelf, n2)
                                v.wait_ge(vSelf, n1)
                                mm = v.tensor_add(dst, dst, t1b[k % 2][:])
                            if k == 23:
                                mm.then_inc(s_qkraw, 1)
                        return op

                    def v_buckets(g_):
                        bks = [[] for _ in range(NCH)]
                        for k in range(24):
                            bks[min(k // 2 + 1, NCH - 1)].append(mk_rope(g_, k))
                        return bks

                    for op in [o for bk in v_buckets(0) for o in bk]:
                        op()

                    for g in range(NKV):
                        buckets = v_buckets(g + 1) if g + 1 < NKV else [[] for _ in range(NCH)]
                        for i in range(NCH):
                            Q = g * NCH + i
                            if i == 0 and g >= 2:
                                v.wait_ge(s_og, 64 * (g - 1))
                            v.wait_ge(s_rb, Q + 1)
                            v.tensor_mul(
                                outg[g % 2][:, i, :, :].rearrange("p a b -> p (a b)"),
                                av_sb[Q % 2][:], ps_av[Q % 2][:],
                            ).then_inc(s_N, 1)
                            for op in buckets[i]:
                                op()
                    phase_inc(v)
        PH["k"] += 1

        s_or = E(nc.semaphore("s_or"))
        s_w4 = [E(nc.semaphore(f"s_w4{i}")) for i in range(3)]
        s_xt4 = [E(nc.semaphore(f"s_xt4{i}")) for i in range(2)]
        s_mt4 = E(nc.semaphore("s_mt4"))
        s_ev4 = E(nc.semaphore("s_ev4"))
        s_pw4 = [E(nc.semaphore(f"s_pw4{i}")) for i in range(2)]
        with ExitStack() as ph:
            PE = ph.enter_context
            out_res = PE(nc.sbuf_tensor("out_res", [128, DT, NOWN], BF16))
            wocol = [PE(nc.sbuf_tensor(f"wocol{i}", [128, DT, 128], BF16)) for i in range(3)]
            xt_t = [PE(nc.sbuf_tensor(f"xt_t{i}", [128, NOWN], F32)) for i in range(2)]
            stage_o = [PE(nc.sbuf_tensor(f"stage_o{i}", [128, NOWN], F32)) for i in range(2)]
            tmp_u = [PE(nc.sbuf_tensor(f"tmp_u{i}", [128, 512], F32)) for i in range(2)]
            ps_o = [PE(nc.psum_tensor(f"ps_o{i}", [128, 2048], F32)) for i in range(2)]
            with nc.Block() as blk:
                @blk.sync
                def _(s):
                    phase_wait(s)
                    for q4 in range(4):
                        s.dma_start(
                            out_res[:, :, q4 * 512:(q4 + 1) * 512],
                            outT_r[:, :, q4 * 512:(q4 + 1) * 512],
                        ).then_inc(s_or, 16)
                    for mt in range(DT):
                        if mt >= 2:
                            s.wait_ge(s_ev4, mt - 1)
                            s.wait_ge(s_xt4[mt % 2], 16 * (mt // 2))
                        s.dma_start(
                            xt_t[mt % 2][:], xT[mt * 128:(mt + 1) * 128, 0:NOWN]
                        ).then_inc(s_xt4[mt % 2], 16)
                    phase_inc(s)

                @blk.gpsimd
                def _(g):
                    phase_wait(g)
                    for mt in range(DT):
                        if mt >= 3:
                            g.wait_ge(s_mt4, mt - 2)
                            g.wait_ge(s_w4[mt % 3], 16 * (mt // 3))
                        g.dma_start(
                            wocol[mt % 3][:], woT_r[:, :, mt * 128:(mt + 1) * 128]
                        ).then_inc(s_w4[mt % 3], 16)
                    phase_inc(g)

                @blk.tensor
                def _(t):
                    phase_wait(t)
                    t.wait_ge(s_or, 64)
                    for mt in range(DT):
                        t.wait_ge(s_w4[mt % 3], 16 * (mt // 3 + 1))
                        if mt >= 2:
                            t.wait_ge(s_ev4, mt - 1)
                        last = None
                        for dt in range(DT):
                            for nb in range(4):
                                last = t.matmul(
                                    ps_o[mt % 2][:, nb * 512:(nb + 1) * 512],
                                    wocol[mt % 3][:, dt, :],
                                    out_res[:, dt, nb * 512:(nb + 1) * 512],
                                    start=(dt == 0), stop=(dt == DT - 1),
                                )
                        last.then_inc(s_mt4, 1)
                    phase_inc(t)

                @blk.vector
                def _(v):
                    phase_wait(v)
                    for mt in range(DT):
                        v.wait_ge(s_mt4, mt + 1)
                        v.wait_ge(s_xt4[mt % 2], 16 * (mt // 2 + 1))
                        if mt >= 2:
                            v.wait_ge(s_pw4[mt % 2], 16 * (mt // 2))
                        last = None
                        for nb in range(4):
                            cols = slice(nb * 512, (nb + 1) * 512)
                            n1 = vedge(v.tensor_scalar_add(tmp_u[nb % 2][:],
                                                           ps_o[mt % 2][:, cols],
                                                           bsum_sb[:, mt:mt + 1]))
                            v.wait_ge(vSelf, n1)
                            last = v.tensor_add(stage_o[mt % 2][:, cols], tmp_u[nb % 2][:],
                                                xt_t[mt % 2][:, cols])
                        last.then_inc(s_ev4, 1)
                    phase_inc(v)

                @blk.scalar
                def _(sc):
                    phase_wait(sc)
                    for mt in range(DT):
                        sc.wait_ge(s_ev4, mt + 1)
                        if mt >= 2:
                            sc.wait_ge(s_pw4[mt % 2], 16 * (mt // 2))
                        sc.dma_start(
                            projT_scr[mt * 128:(mt + 1) * 128, :], stage_o[mt % 2][:]
                        ).then_inc(s_pw4[mt % 2], 16)
                    sc.wait_ge(s_pw4[0], 16 * (DT // 2))
                    sc.wait_ge(s_pw4[1], 16 * (DT // 2))
                    phase_inc(sc)
        PH["k"] += 1

        s_xs6 = E(nc.semaphore("s_xs6"))
        s_w6 = E(nc.semaphore("s_w6"))
        s_sg6 = E(nc.semaphore("s_sg6"))
        s_os6 = E(nc.semaphore("s_os6"))
        s_oc6 = E(nc.semaphore("s_oc6"))
        s_jt6 = E(nc.semaphore("s_jt6"))
        s_fin6 = E(nc.semaphore("s_fin6"))
        s_pj6 = [E(nc.semaphore(f"s_pj6{i}")) for i in range(2)]
        s_out = [E(nc.semaphore(f"s_out{i}")) for i in range(2)]
        with ExitStack() as ph:
            PE = ph.enter_context
            xs_res = PE(nc.sbuf_tensor("xs_res", [128, DT, NOWN], BF16))
            wcol_f = PE(nc.sbuf_tensor("wcol_f", [128, 16, 128], F32))
            wsign = [PE(nc.sbuf_tensor(f"wsign{i}", [128, DT, 128], BF16)) for i in range(2)]
            os_sb = [PE(nc.sbuf_tensor(f"os_sb{i}", [128, 2048], BF16)) for i in range(2)]
            proj_t = [PE(nc.sbuf_tensor(f"proj_t{i}", [128, NOWN], F32)) for i in range(2)]
            out_st = [PE(nc.sbuf_tensor(f"out_st{i}", [128, NOWN], F32)) for i in range(2)]
            ps_f = [PE(nc.psum_tensor(f"ps_f{i}", [128, 2048], F32)) for i in range(2)]
            with nc.Block() as blk:
                @blk.sync
                def _(s):
                    phase_wait(s)
                    for q4 in range(4):
                        s.dma_start(
                            xs_res[:, :, q4 * 512:(q4 + 1) * 512],
                            xsT_r[:, :, q4 * 512:(q4 + 1) * 512],
                        ).then_inc(s_xs6, 16)
                    for jt in range(DT):
                        if jt >= 2:
                            s.wait_ge(s_fin6, 4 * (jt - 1))
                            s.wait_ge(s_pj6[jt % 2], 16 * (jt // 2))
                        s.dma_start(
                            proj_t[jt % 2][:], projT_scr[jt * 128:(jt + 1) * 128, :]
                        ).then_inc(s_pj6[jt % 2], 16)
                    s.wait_ge(s_out[0], 16 * (DT // 2))
                    s.wait_ge(s_out[1], 16 * (DT // 2))
                    phase_inc(s)

                @blk.gpsimd
                def _(g):
                    phase_wait(g)
                    for jt in range(DT):
                        for hf in range(2):
                            k = 2 * jt + hf
                            if k >= 1:
                                g.wait_ge(s_sg6, k)
                                g.wait_ge(s_w6, 16 * k)
                            g.dma_start(
                                wcol_f[:],
                                wfT_r[:, 16 * hf:16 * (hf + 1), jt * 128:(jt + 1) * 128],
                            ).then_inc(s_w6, 16)
                    phase_inc(g)

                @blk.scalar
                def _(sc):
                    phase_wait(sc)
                    for jt in range(DT):
                        sc.wait_ge(s_os6, jt + 1)
                        last = None
                        for nb in range(4):
                            last = sc.copy(os_sb[jt % 2][:, nb * 512:(nb + 1) * 512],
                                           ps_f[jt % 2][:, nb * 512:(nb + 1) * 512])
                        last.then_inc(s_oc6, 1)
                        for hf in range(2):
                            sc.wait_ge(s_w6, 16 * (2 * jt + hf + 1))
                            if jt >= 2 and hf == 0:
                                sc.wait_ge(s_jt6, jt - 1)
                            sc.activation(wsign[jt % 2][:, 16 * hf:16 * (hf + 1), :],
                                          wcol_f[:], AF.Sign).then_inc(s_sg6, 1)
                        if jt >= 1:
                            sc.wait_ge(s_fin6, 4 * jt)
                            if jt >= 3:
                                sc.wait_ge(s_out[(jt - 1) % 2], 16 * ((jt - 1) // 2))
                            sc.dma_start(
                                yT[(jt - 1) * 128:jt * 128, :], out_st[(jt - 1) % 2][:]
                            ).then_inc(s_out[(jt - 1) % 2], 16)
                    sc.wait_ge(s_fin6, 4 * DT)
                    sc.wait_ge(s_out[(DT - 1) % 2], 16 * ((DT - 1) // 2))
                    sc.dma_start(
                        yT[(DT - 1) * 128:DT * 128, :], out_st[(DT - 1) % 2][:]
                    ).then_inc(s_out[(DT - 1) % 2], 16)
                    phase_inc(sc)

                @blk.tensor
                def _(t):
                    phase_wait(t)
                    t.wait_ge(s_xs6, 64)

                    def os_block(jj):
                        t.wait_ge(s_fin6, 4 * (jj - 1)) if jj >= 1 else None
                        last = None
                        for nb in range(4):
                            last = t.matmul(ps_f[jj % 2][:, nb * 512:(nb + 1) * 512],
                                            ocs_sb[0:8, jj * 128:(jj + 1) * 128],
                                            routT_sb[:, nb * 512:(nb + 1) * 512],
                                            start=True, stop=True)
                        last.then_inc(s_os6, 1)

                    os_block(0)
                    for jt in range(DT):
                        t.wait_ge(s_oc6, jt + 1)
                        t.wait_ge(s_sg6, 2 * (jt + 1))
                        last = None
                        for dt in range(DT):
                            if dt == 16 and jt + 1 < DT:
                                os_block(jt + 1)
                            for nb in range(4):
                                last = t.matmul(
                                    ps_f[jt % 2][:, nb * 512:(nb + 1) * 512],
                                    wsign[jt % 2][:, dt, :],
                                    xs_res[:, dt, nb * 512:(nb + 1) * 512],
                                    start=(dt == 0), stop=(dt == DT - 1),
                                )
                        last.then_inc(s_jt6, 1)
                    phase_inc(t)

                @blk.vector
                def _(v):
                    phase_wait(v)
                    for jt in range(DT):
                        v.wait_ge(s_jt6, jt + 1)
                        v.wait_ge(s_pj6[jt % 2], 16 * (jt // 2 + 1))
                        if jt >= 2:
                            v.wait_ge(s_out[jt % 2], 16 * (jt // 2))
                        for nb in range(4):
                            cols = slice(nb * 512, (nb + 1) * 512)
                            n1 = vedge(v.tensor_mul(out_st[jt % 2][:, cols],
                                                    ps_f[jt % 2][:, cols],
                                                    os_sb[jt % 2][:, cols]))
                            v.wait_ge(vSelf, n1)
                            v.tensor_add(out_st[jt % 2][:, cols], out_st[jt % 2][:, cols],
                                         proj_t[jt % 2][:, cols]).then_inc(s_fin6, 1)
                    phase_inc(v)
        PH["k"] += 1

    return nc


def _host_prep(
    L0_attn_x, L0_attn_w_qkv, L0_attn_b_qkv, L0_attn_w_o, L0_attn_b_o,
    L0_ffn_weight, L0_ffn_bias, L0_ffn_gate_w, L0_ffn_in_channel_scale,
    L0_ffn_out_channel_scale,
):
    x = np.asarray(L0_attn_x, np.float32)
    wq = np.asarray(L0_attn_w_qkv, np.float32)
    bq = np.asarray(L0_attn_b_qkv, np.float32)
    wo = np.asarray(L0_attn_w_o, np.float32)
    bo_ = np.asarray(L0_attn_b_o, np.float32)
    wf = np.asarray(L0_ffn_weight, np.float32)
    bf_ = np.asarray(L0_ffn_bias, np.float32)
    gw = np.asarray(L0_ffn_gate_w, np.float32)
    ics = np.asarray(L0_ffn_in_channel_scale, np.float32)
    ocs = np.asarray(L0_ffn_out_channel_scale, np.float32)

    sigma = np.concatenate([np.arange(0, HD, 2), np.arange(1, HD, 2)])
    qk_perm = np.arange(6144)
    for hh in range(40):
        qk_perm[hh * HD:(hh + 1) * HD] = hh * HD + sigma
    wqT = np.ascontiguousarray(wq.T[:, qk_perm])
    woT = np.ascontiguousarray(wo.T)
    wfT = np.ascontiguousarray(wf.T)
    gwT = np.ascontiguousarray(gw.T)
    bqkv_t = np.ascontiguousarray(bq[qk_perm].reshape(JW, 128).T)
    bo_t = np.ascontiguousarray(bo_.reshape(DT, 128).T)
    bf_t = np.ascontiguousarray(bf_.reshape(DT, 128).T)

    iden_m = np.eye(128, dtype=ml_dtypes.bfloat16)
    perm_m = np.zeros((128, 128), dtype=ml_dtypes.bfloat16)
    for i in range(64):
        perm_m[2 * i, 2 * i + 1] = 1
        perm_m[2 * i + 1, 2 * i] = 1
    onec_m = np.ones((128, 1), dtype=ml_dtypes.bfloat16)
    onerow_m = np.ones((1, 128), dtype=np.float32)
    onerow_bf_m = np.ones((1, 128), dtype=ml_dtypes.bfloat16)
    tri = np.where(np.arange(128)[:, None] <= np.arange(128)[None, :], 0.0, MBIG)
    mtri_m = np.tile(tri, (1, 4)).astype(ml_dtypes.bfloat16)

    idx = np.arange(S).reshape(32, 128)
    inv_freq = 1.0 / (10000.0 ** (np.arange(0, HD, 2, dtype=np.float64) / HD))

    in_maps = []
    for c in range(8):
        b, p = divmod(c, 2)
        order = np.concatenate([idx[p::2].ravel(), idx[1 - p::2].ravel()])
        xT_store = np.ascontiguousarray(x[b].T[:, order])
        ang = order[None, :].astype(np.float64) * inv_freq[:, None]
        cos_t = np.empty((128, S), np.float64)
        sin_t = np.empty((128, S), np.float64)
        cos_t[0:64] = np.cos(ang)
        cos_t[64:128] = np.cos(ang)
        sin_t[0:64] = -np.sin(ang)
        sin_t[64:128] = np.sin(ang)
        moth_m = np.full((128, 512), MBIG if p == 0 else 0.0, dtype=ml_dtypes.bfloat16)
        in_maps.append({
            "xT": xT_store, "wqT": wqT, "woT": woT, "wfT": wfT, "gwT": gwT,
            "ics8": ics, "ocs8": ocs, "bqkv": bqkv_t, "bo": bo_t, "bff": bf_t,
            "cost": cos_t.astype(np.float32), "sint": sin_t.astype(np.float32),
            "mtri": mtri_m, "moth": moth_m, "perm": perm_m, "iden": iden_m,
            "onec": onec_m, "onerow": onerow_m, "onerow_bf": onerow_bf_m,
        })
    return in_maps


def kernel(**inputs):
    in_maps = _host_prep(**inputs)
    nc = _build()
    res = run_bass_kernel_spmd(nc, in_maps, core_ids=list(range(8)))
    idx = np.arange(S).reshape(32, 128)
    out = np.empty((B, S, D), np.float32)
    for c in range(8):
        b, p = divmod(c, 2)
        own = idx[p::2].ravel()
        out[b][own, :] = res.results[c]["yT"].T
    return out
